# version 05
# baseline (speedup 1.0000x reference)
import dataclasses
from contextlib import ExitStack

import numpy as np
import ml_dtypes

import concourse.bass as bass
import concourse.mybir as mybir
from concourse.bass_utils import run_bass_kernel_spmd

F32 = mybir.dt.float32
BF16 = mybir.dt.bfloat16
I32 = mybir.dt.int32
AF = mybir.ActivationFunctionType
ALU = mybir.AluOpType
AX = mybir.AxisListType
bfloat16 = ml_dtypes.bfloat16

L = 16384
D = 1024
NCORE = 8
IFF = 2816
C = 16
TB = 2048
NBLK = L // TB
EXP_MASK = 0x7F800000
SIGN_MASK = int(np.uint32(0x80000000).astype(np.int32))
ENGS = ("sync", "vector", "scalar", "tensor", "gpsimd")


class Sem:
    def __init__(self, h):
        self.h = h
        self.n = 0


class Plan:

    def __init__(self):
        self.ops = {e: [] for e in ENGS}
        self.pending = {}

    def _flush(self, sem):
        p = self.pending.get(sem)
        if p is not None:
            eng, cnt, rec = p
            sem.n += cnt
            rec[1] = (sem, cnt)
            self.pending[sem] = None

    def wait(self, eng, sem, val=None):
        if val is None:
            self._flush(sem)
            v = sem.n
        else:
            v = val
            if v > sem.n:
                self._flush(sem)
                v = min(v, sem.n)
        if v > 0:
            self.ops[eng].append([lambda e, s=sem, v=v: e.wait_ge(s.h, v), None, True])

    def op(self, eng, fn, sem=None, k=1):
        rec = [fn, None, False]
        self.ops[eng].append(rec)
        if sem is not None:
            if k == 16:
                sem.n += k
                rec[1] = (sem, k)
            else:
                p = self.pending.get(sem)
                if p is not None and p[0] == eng:
                    self.pending[sem] = (eng, p[1] + k, rec)
                else:
                    self._flush(sem)
                    self.pending[sem] = (eng, k, rec)

    def mark(self, sem):
        self._flush(sem)
        return sem.n

    def replay(self, eng, e):
        for rec in self.ops[eng]:
            fn, inc, is_wait = rec
            if is_wait:
                fn(e)
            else:
                instr = fn(e)
                if inc is not None:
                    instr.then_inc(inc[0].h, inc[1])


def view(ap2d, extra_off, dims):
    a = ap2d.copy()
    newap = [list(a.ap[0])] + [list(x) for x in dims]
    return dataclasses.replace(a, offset=a.offset + extra_off, ap=newap)


PARTS = {"attn": True, "ffn": True, "bias": True}


def build_program(debug=False):
    nc = bass.Bass("TRN2", target_bir_lowering=False)
    es = ExitStack()

    def mk(stack):
        def sb(name, shape, dtype):
            return stack.enter_context(nc.sbuf_tensor(name, shape, dtype))
        return sb

    sb = mk(es)

    def sem(name):
        return Sem(es.enter_context(nc.semaphore(name)))

    XT = nc.dram_tensor("xt", [D, L], F32, kind="ExternalInput").ap()
    XTO = nc.dram_tensor("xto", [D, TB], F32, kind="ExternalInput").ap()
    WQKVT = nc.dram_tensor("wqkvt", [D, 384], F32, kind="ExternalInput").ap()
    BQKV = nc.dram_tensor("bqkv", [1, 384], F32, kind="ExternalInput").ap()
    OUTWT = nc.dram_tensor("outwt", [D, D], F32, kind="ExternalInput").ap()
    OUTB = nc.dram_tensor("outb", [1, D], F32, kind="ExternalInput").ap()
    LAM_IJ = nc.dram_tensor("lam_ij", [128, 64], F32, kind="ExternalInput").ap()
    GAM_IJ = nc.dram_tensor("gam_ij", [128, 64], F32, kind="ExternalInput").ap()
    LAM_JI = nc.dram_tensor("lam_ji", [128, 64], F32, kind="ExternalInput").ap()
    GAM_JI = nc.dram_tensor("gam_ji", [128, 64], F32, kind="ExternalInput").ap()
    WSTAT_JI = nc.dram_tensor("wstat_ji", [128, 64], F32, kind="ExternalInput").ap()
    WG_SL = nc.dram_tensor("wg_sl", [352, D], F32, kind="ExternalInput").ap()
    WU_SL = nc.dram_tensor("wu_sl", [352, D], F32, kind="ExternalInput").ap()
    WD_SL = nc.dram_tensor("wd_sl", [128, IFF], F32, kind="ExternalInput").ap()
    IDENT = nc.dram_tensor("ident", [128, 128], F32, kind="ExternalInput").ap()
    OUT = nc.dram_tensor("outx", [D, TB], F32, kind="ExternalOutput").ap()

    YSLICE = nc.dram_tensor("yslice", [128, L], BF16).ap()
    YGATH = nc.dram_tensor("ygath", [NCORE * 128, L], BF16).ap()
    WGQ_SL = nc.dram_tensor("wgq_sl", [D, 352], BF16).ap()
    WUQ_SL = nc.dram_tensor("wuq_sl", [D, 352], BF16).ap()
    WDQ_SL = nc.dram_tensor("wdq_sl", [IFF, 128], BF16).ap()
    WGQ = nc.dram_tensor("wgq", [NCORE * D, 352], BF16).ap()
    WUQ = nc.dram_tensor("wuq", [NCORE * D, 352], BF16).ap()
    WDQ = nc.dram_tensor("wdq", [NCORE * IFF, 128], BF16).ap()
    RFLAT = nc.dram_tensor("rflat", [4, 8192], F32).ap()
    MRHO = nc.dram_tensor("mrho", [2, TB], F32).ap()
    OWB16 = nc.dram_tensor("owb16", [D, D], BF16).ap()
    dbg = {}
    if debug:
        dbg["qkv"] = nc.dram_tensor("dbg_qkv", [384, L], BF16, kind="ExternalOutput").ap()
        dbg["sig"] = nc.dram_tensor("dbg_sig", [NBLK * 128, 8192], BF16, kind="ExternalOutput").ap()
        dbg["y"] = nc.dram_tensor("dbg_y", [128, L], BF16, kind="ExternalOutput").ap()
        dbg["xq"] = nc.dram_tensor("dbg_xq", [D, TB], BF16, kind="ExternalOutput").ap()
        dbg["hq"] = nc.dram_tensor("dbg_hq", [128, 24 * 256], BF16, kind="ExternalOutput").ap()
        dbg["ygt"] = nc.dram_tensor("dbg_ygt", [128, 8 * 256], BF16, kind="ExternalOutput").ap()
        dbg["wdq"] = nc.dram_tensor("dbg_wdq", [IFF, 128], BF16, kind="ExternalOutput").ap()

    scr = [sb(f"scr{i}", [128, 8192], BF16) for i in range(2)]
    xf = [sb(f"xf{i}", [128, 1024], F32) for i in range(2)]
    qsf = sb("qsf", [128, 2048], F32)
    wqb = sb("wqb", [128, 8 * 384], BF16)
    identt = sb("identt", [128, 128], BF16)
    identf = sb("identf", [128, 128], F32)
    onesrow = sb("onesrow", [1, 512], BF16)
    onescol = sb("onescol", [128, 1], BF16)
    rowsum = sb("rowsum", [1, 384], F32)
    rowsum_nb = sb("rowsum_nb", [1, 384], BF16)
    bqkv_b = sb("bqkv_b", [1, 384], BF16)
    murow = sb("murow", [1, TB], F32)
    mub = sb("mub", [1, TB], BF16)
    rhorow = sb("rhorow", [1, TB], F32)
    rhob = sb("rhob", [1, TB], BF16)
    isb_ = sb("isb_", [1, TB], BF16)
    rhobc = sb("rhobc", [128, TB], BF16)
    qs1 = sb("qs1", [128, 128], F32)
    qsc = sb("qsc", [128, 128], F32)
    fence_s = sb("fence_s", [1, 4], F32)
    outbb = sb("outbb", [1, D], BF16)
    sq5 = [sb(f"sq5{i}", [128, 512], BF16) for i in range(2)]
    epsr = sb("epsr", [1, 1], F32)

    sl = sem("sl")
    sl2 = sem("sl2")
    sl3 = sem("sl3")
    sv = sem("sv")
    sa = sem("sa")
    sp = sem("sp")
    st_ = sem("st")
    scc = sem("scc")

    def psum(name, shape, dtype=F32, stack=None):
        return (stack or es).enter_context(nc.psum_tensor(name, shape, dtype))

    ps_r1 = psum("ps_r1", [1, 512], F32)
    ps_r2 = psum("ps_r2", [1, 512], F32)
    ps_mm = psum("ps_mm", [128, 512], F32)
    ps_y = psum("ps_y", [128, 512], F32)
    ps_tr = psum("ps_tr", [128, 128], BF16)
    ps_tr2 = psum("ps_tr2", [128, 128], BF16)

    P = Plan()

    def bar():
        for _e in ENGS:
            for _s in (sl, sl2, sl3, sv, sa, sp, st_):
                P.wait(_e, _s)

    def vfence():
        P.wait("vector", sv)
        P.op("vector", lambda e: e.tensor_copy(out=fence_s[0:1, 0:1], in_=epsr[0:1, 0:1]), sv)

    def sfence():
        P.wait("scalar", sa)
        P.op("scalar", lambda e: e.activation(out=fence_s[0:1, 1:2], in_=epsr[0:1, 0:1],
                                              func=AF.Copy), sa)

    lnt1 = qsf[0:1, 0:2048]
    QXN = qsf[:, 0:1024]
    QB_ = qsf[:, 1024:2048]

    esP = ExitStack()
    sbP = mk(esP)
    r_ij = sbP("r_ij", [128, 64], F32)
    rc_ij = sbP("rc_ij", [128, 64], F32)
    gr_ij = sbP("gr_ij", [128, 64], F32)
    r_ji = sbP("r_ji", [128, 64], F32)
    gam_ji = sbP("gam_ji_s", [128, 64], F32)
    wst_ji = sbP("wst_ji_s", [128, 64], F32)
    ptmp = sbP("ptmp", [128, 64], F32)
    rcmir_in = sbP("rcmir_in", [64, 128], F32)
    qtbP = sbP("qtbP", [128, 1024], BF16)
    qsgP = sbP("qsgP", [128, 1024], F32)
    wqtmp = sbP("wqtmp", [128, 1024], BF16)

    P.op("sync", lambda e: e.dma_start(out=identf[:], in_=IDENT), sl, 16)
    P.op("sync", lambda e: e.dma_start(out=r_ij[:], in_=LAM_IJ), sl, 16)
    P.op("sync", lambda e: e.dma_start(out=gr_ij[:], in_=GAM_IJ), sl, 16)
    P.op("sync", lambda e: e.dma_start(out=r_ji[:], in_=LAM_JI), sl, 16)
    P.op("sync", lambda e: e.dma_start(out=gam_ji[:], in_=GAM_JI), sl, 16)
    P.op("sync", lambda e: e.dma_start(out=wst_ji[:], in_=WSTAT_JI), sl, 16)
    P.op("sync", lambda e: e.dma_start(out=rowsum[:], in_=BQKV), sl, 16)
    P.op("sync", lambda e: e.dma_start(out=lnt1[0:1, 0:1024], in_=OUTB), sl, 16)
    P.wait("scalar", sl)
    P.op("scalar", lambda e: e.activation(out=r_ij[:], in_=r_ij[:], func=AF.Sigmoid), sa)
    P.op("scalar", lambda e: e.activation(out=r_ji[:], in_=r_ji[:], func=AF.Sigmoid), sa)
    P.wait("vector", sa)
    P.op("vector", lambda e: e.tensor_scalar(out=r_ij[:], in0=r_ij[:], scalar1=-1.0, scalar2=1.0,
                                             op0=ALU.mult, op1=ALU.add), sv)
    P.op("vector", lambda e: e.tensor_scalar(out=r_ji[:], in0=r_ji[:], scalar1=-1.0, scalar2=1.0,
                                             op0=ALU.mult, op1=ALU.add), sv)
    P.op("vector", lambda e: e.tensor_copy(out=identt[:], in_=identf[:]), sv)
    P.op("vector", lambda e: e.tensor_copy(out=bqkv_b[:], in_=rowsum[:]), sv)
    P.op("vector", lambda e: e.tensor_copy(out=outbb[:], in_=lnt1[0:1, 0:1024]), sv)
    P.op("vector", lambda e: e.memset(epsr[:], 1e-5), sv)
    P.op("vector", lambda e: e.memset(onesrow[:], 1.0), sv)
    P.op("vector", lambda e: e.memset(onescol[:], 1.0), sv)

    esA = ExitStack()
    sbA = mk(esA)
    qb = sbA("qb", [128, TB], BF16)
    kb = sbA("kb", [128, TB], BF16)
    vb = sbA("vb", [128, TB], BF16)
    qP = sbA("qP", [128, C * 128], BF16)
    kP = sbA("kP", [128, C * 128], BF16)
    vP = sbA("vP", [128, C * 128], BF16)
    rbc = sbA("rbc", [128, 8192], BF16)
    anc0 = sbA("anc0", [128, 8192], BF16)
    sigm = sbA("sigm", [128, 8192], BF16)
    z2s = sbA("z2s", [128, C * 128], BF16)
    ysb = sbA("ysb", [128, TB], F32)
    pn_tab = [sbA(f"pn{n}", [128, 128], BF16) for n in range(C)]
    wst_bd = sbA("wst_bd", [128, 128], BF16)
    rcmir = sbA("rcmir", [128, 64], F32)
    carry = sbA("carry", [128, 64], F32)
    sigext = sbA("sigext", [128, 136], BF16)
    sigf = sbA("sigf", [128, 128], F32)
    ps_z = psum("ps_z", [128, 512], F32, esA)
    ps_trB = psum("ps_trB", [128, 128], BF16, esA)

    P.op("vector", lambda e: e.memset(carry[:], 0.0), sv)
    P.op("vector", lambda e: e.tensor_copy(out=ptmp[:], in_=gam_ji[:]), sv)
    for n in range(C):
        P.op("vector", lambda e, n=n: e.memset(pn_tab[n][:], 0.0), sv)
        P.op("vector", lambda e, n=n: e.tensor_copy(out=pn_tab[n][0:64, 0:64], in_=ptmp[0:64, :]), sv)
        P.op("vector", lambda e, n=n: e.tensor_copy(out=pn_tab[n][64:128, 64:128], in_=ptmp[64:128, :]), sv)
        if n + 1 < C:
            P.op("vector", lambda e: e.tensor_tensor(out=ptmp[:], in0=ptmp[:], in1=r_ji[:], op=ALU.mult), sv)
    P.op("vector", lambda e: e.memset(wst_bd[:], 0.0), sv)
    P.op("vector", lambda e: e.tensor_copy(out=wst_bd[0:64, 0:64], in_=wst_ji[0:64, :]), sv)
    P.op("vector", lambda e: e.tensor_copy(out=wst_bd[64:128, 64:128], in_=wst_ji[64:128, :]), sv)
    P.op("vector", lambda e: e.tensor_tensor(out=rc_ij[:], in0=r_ij[:], in1=r_ij[:], op=ALU.mult), sv)
    for _ in range(3):
        P.op("vector", lambda e: e.tensor_tensor(out=rc_ij[:], in0=rc_ij[:], in1=rc_ij[:], op=ALU.mult), sv)
    P.op("vector", lambda e: e.tensor_tensor(out=gr_ij[:], in0=gr_ij[:], in1=r_ij[:], op=ALU.mult), sv)
    P.wait("gpsimd", sv)
    P.op("gpsimd", lambda e: e.dma_start(
        out=RFLAT[0:1, :].rearrange("one (p f) -> (one p) f", p=128), in_=r_ij[:]), sl2, 16)
    P.op("gpsimd", lambda e: e.dma_start(
        out=RFLAT[1:2, :].rearrange("one (p f) -> (one p) f", p=128), in_=gr_ij[:]), sl2, 16)
    P.op("gpsimd", lambda e: e.dma_start(
        out=RFLAT[3:4, :].rearrange("one (p f) -> (one p) f", p=128), in_=rc_ij[:]), sl2, 16)
    P.wait("gpsimd", sl2)
    P.op("gpsimd", lambda e: e.dma_start(out=rbc[:], in_=RFLAT[0:1, :].broadcast_to([128, 8192])), sl2, 16)
    P.op("gpsimd", lambda e: e.dma_start(out=anc0[:], in_=RFLAT[1:2, :].broadcast_to([128, 8192])), sl2, 16)
    P.op("gpsimd", lambda e: e.dma_start(
        out=rcmir_in[:], in_=RFLAT[3:4, :].rearrange("one (a b) -> (one a) b", b=128)), sl2, 16)
    P.wait("tensor", sl2)
    P.wait("tensor", sv)
    P.op("tensor", lambda e: e.matmul(out=ps_mm[0:128, 0:64], lhsT=rcmir_in[:, 0:128],
                                      rhs=identf[0:64, 0:64], start=True, stop=True), sp)
    P.wait("vector", sp)
    P.wait("vector", sl2)
    P.op("vector", lambda e: e.tensor_copy(out=rcmir[:], in_=ps_mm[0:128, 0:64]), sv)

    for dmc in range(8):
        P.wait("sync", sv)
        P.op("sync", lambda e, dmc=dmc: e.dma_start(
            out=xf[0][:, 0:384], in_=WQKVT[dmc * 128:(dmc + 1) * 128, :]), sl, 16)
        P.wait("vector", sl)
        P.op("vector", lambda e, dmc=dmc: e.tensor_copy(
            out=wqb[:, dmc * 384:(dmc + 1) * 384], in_=xf[0][:, 0:384]), sv)
    for dmc in range(8):
        P.wait("sync", sv)
        P.op("sync", lambda e, dmc=dmc: e.dma_start(
            out=xf[0][:], in_=OUTWT[dmc * 128:(dmc + 1) * 128, :]), sl, 16)
        P.wait("vector", sl)
        P.op("vector", lambda e: e.tensor_copy(out=qtbP[:], in_=xf[0][:]), sv)
        P.wait("gpsimd", sv)
        P.op("gpsimd", lambda e, dmc=dmc: e.dma_start(
            out=OWB16[dmc * 128:(dmc + 1) * 128, :], in_=qtbP[:]), sl2, 16)
        P.wait("vector", sl2)
    P.wait("tensor", sv)
    for dmc in range(8):
        P.op("tensor", lambda e, dmc=dmc: e.matmul(
            out=ps_r1[0:1, 0:384], lhsT=onescol[:], rhs=wqb[:, dmc * 384:(dmc + 1) * 384],
            start=(dmc == 0), stop=(dmc == 7)), sp)
    P.wait("vector", sp)
    P.op("vector", lambda e: e.tensor_copy(out=rowsum[:], in_=ps_r1[0:1, 0:384]), sv)
    P.op("vector", lambda e: e.tensor_scalar(out=rowsum_nb[:], in0=rowsum[:], scalar1=-1.0,
                                             scalar2=None, op0=ALU.mult), sv)

    def quant_tile(src, dst, pcnt, ncols, qtb, qsg):
        nb16 = ncols // 16
        P.wait("vector", sv)
        P.op("vector", lambda e: e.tensor_reduce(
            out=view(qs1[:], 0, [[1, nb16], [1, 1]])[0:pcnt],
            in_=view(src, 0, [[16, nb16], [1, 16]])[0:pcnt],
            axis=AX.X, op=ALU.max, apply_absolute_value=True), sv)
        P.wait("vector", sv)
        P.op("vector", lambda e: e.tensor_scalar(
            out=qs1[0:pcnt, 0:nb16], in0=qs1[0:pcnt, 0:nb16],
            scalar1=1.0 / 6.0, scalar2=1e-12, op0=ALU.mult, op1=ALU.max), sv)
        P.wait("vector", sv)
        P.op("vector", lambda e: e.reciprocal(out=qsc[0:pcnt, 0:nb16], in_=qs1[0:pcnt, 0:nb16]), sv)
        P.wait("vector", sv)
        P.op("vector", lambda e: e.tensor_tensor(
            out=view(QXN, 0, [[16, nb16], [1, 16]])[0:pcnt],
            in0=view(src, 0, [[16, nb16], [1, 16]])[0:pcnt],
            in1=view(qsc[:], 0, [[1, nb16], [0, 16]])[0:pcnt], op=ALU.mult), sv)
        P.wait("vector", sv)
        P.wait("vector", sv)
        P.op("vector", lambda e: e.tensor_scalar(
            out=QB_[0:pcnt, 0:ncols].bitcast(I32), in0=QXN[0:pcnt, 0:ncols].bitcast(I32),
            scalar1=EXP_MASK, scalar2=None, op0=ALU.bitwise_and), sv)
        P.wait("vector", sv)
        P.op("vector", lambda e: e.tensor_scalar(
            out=QB_[0:pcnt, 0:ncols], in0=QB_[0:pcnt, 0:ncols],
            scalar1=1.0, scalar2=64.0, op0=ALU.max, op1=ALU.mult), sv)
        P.op("vector", lambda e: e.tensor_scalar(
            out=qsg[0:pcnt, 0:ncols].bitcast(I32), in0=QXN[0:pcnt, 0:ncols].bitcast(I32),
            scalar1=SIGN_MASK, scalar2=None, op0=ALU.bitwise_and), sv)
        P.wait("vector", sv)
        P.op("vector", lambda e: e.tensor_tensor(
            out=QB_[0:pcnt, 0:ncols].bitcast(I32), in0=QB_[0:pcnt, 0:ncols].bitcast(I32),
            in1=qsg[0:pcnt, 0:ncols].bitcast(I32), op=ALU.bitwise_or), sv)
        P.wait("vector", sv)
        P.op("vector", lambda e: e.tensor_tensor(
            out=qtb[0:pcnt, 0:ncols], in0=QXN[0:pcnt, 0:ncols], in1=QB_[0:pcnt, 0:ncols],
            op=ALU.add), sv)
        P.wait("vector", sv)
        P.op("vector", lambda e: e.tensor_tensor(
            out=qtb[0:pcnt, 0:ncols], in0=qtb[0:pcnt, 0:ncols], in1=QB_[0:pcnt, 0:ncols],
            op=ALU.subtract), sv)
        P.wait("vector", sv)
        P.op("vector", lambda e: e.tensor_tensor(
            out=view(dst, 0, [[16, nb16], [1, 16]])[0:pcnt],
            in0=view(qtb[:], 0, [[16, nb16], [1, 16]])[0:pcnt],
            in1=view(qs1[:], 0, [[1, nb16], [0, 16]])[0:pcnt], op=ALU.mult), sv)
        P.wait("vector", sv)

    def wquant(SRC, DST, prow0, pcnt, col0, cw, dstc0):
        P.wait("sync", sv)
        P.op("sync", lambda e: e.dma_start(
            out=xf[0][0:pcnt, 0:cw].bitcast(F32) if False else xf[0][0:pcnt, 0:cw],
            in_=SRC[prow0:prow0 + pcnt, col0:col0 + cw]), sl, 16)
        P.wait("vector", sl)
        quant_tile(xf[0][:, 0:cw], wqtmp[:, 0:cw], pcnt, cw, qtbP, qsgP)
        P.wait("tensor", sv)
        for fc in range(cw // 128):
            trp = ps_tr if fc % 2 == 0 else ps_trB
            P.op("tensor", lambda e, fc=fc, trp=trp: e.transpose(
                out=trp[:], in_=wqtmp[0:128, fc * 128:(fc + 1) * 128], identity=identt[:]), sp)
            P.wait("vector", sp)
            P.op("vector", lambda e, fc=fc, trp=trp: e.tensor_copy(
                out=qtbP[:, 0:128], in_=trp[:, 0:128]), sv)
            P.wait("gpsimd", sv)
            P.op("gpsimd", lambda e, fc=fc: e.dma_start(
                out=DST[col0 + fc * 128:col0 + (fc + 1) * 128, dstc0:dstc0 + pcnt],
                in_=qtbP[0:128, 0:pcnt]), sl2, 16)
            P.wait("vector", sl2)
            P.wait("tensor", sl2, max(0, sl2.n - 16))

    for (SRC, DST) in ((WG_SL, WGQ_SL), (WU_SL, WUQ_SL)):
        for pc, pcnt in ((0, 128), (1, 128), (2, 96)):
            wquant(SRC, DST, pc * 128, pcnt, 0, 1024, pc * 128)
    for col0, cw in ((0, 1024), (1024, 1024), (2048, 768)):
        wquant(WD_SL, WDQ_SL, 0, 128, col0, cw, 0)
    P.wait("gpsimd", sl2)
    P.wait("tensor", sl2)
    P.op("gpsimd", lambda e: e.collective_compute(
        "AllGather", ALU.bypass, replica_groups=[list(range(NCORE))],
        ins=[WGQ_SL.opt()], outs=[WGQ.opt()]), scc, 1)
    P.mark(scc)
    P.op("gpsimd", lambda e: e.collective_compute(
        "AllGather", ALU.bypass, replica_groups=[list(range(NCORE))],
        ins=[WUQ_SL.opt()], outs=[WUQ.opt()]), scc, 1)
    P.mark(scc)
    P.op("gpsimd", lambda e: e.collective_compute(
        "AllGather", ALU.bypass, replica_groups=[list(range(NCORE))],
        ins=[WDQ_SL.opt()], outs=[WDQ.opt()]), scc, 1)
    P.mark(scc)
    bar()

    def emit_stats(xsrc_dram):
        P.wait("scalar", sp)
        P.wait("scalar", sv)
        P.wait("sync", sv)
        P.wait("sync", sa)
        for dmc in range(8):
            for hh in range(2):
                if dmc + hh > 0:
                    P.wait("sync", sa)
                P.op("sync", lambda e, dmc=dmc, hh=hh: e.dma_start(
                    out=xf[(2 * dmc + hh) % 2][:], in_=xsrc_dram(dmc, hh)), sl, 16)
                P.wait("scalar", sl)
                P.op("scalar", lambda e, dmc=dmc, hh=hh: e.activation(
                    out=scr[dmc // 4][:, (dmc % 4) * 2048 + hh * 1024:(dmc % 4) * 2048 + hh * 1024 + 1024],
                    in_=xf[(2 * dmc + hh) % 2][:], func=AF.Copy), sa)
        for nb in range(4):
            P.wait("tensor", sa)
            for dmc in range(8):
                off = (dmc % 4) * 2048 + nb * 512
                P.wait("scalar", sp)
                P.op("scalar", lambda e, dmc=dmc, off=off: e.activation(
                    out=sq5[dmc % 2][:], in_=scr[dmc // 4][:, off:off + 512], func=AF.Square), sa)
                P.op("tensor", lambda e, dmc=dmc, off=off: e.matmul(
                    out=ps_r1[0:1, 0:512], lhsT=onescol[:], rhs=scr[dmc // 4][:, off:off + 512],
                    start=(dmc == 0), stop=(dmc == 7)), sp)
                P.wait("tensor", sa)
                P.op("tensor", lambda e, dmc=dmc: e.matmul(
                    out=ps_r2[0:1, 0:512], lhsT=onescol[:], rhs=sq5[dmc % 2][:],
                    start=(dmc == 0), stop=(dmc == 7)), sp)
            P.wait("vector", sp)
            cb = slice(nb * 512, nb * 512 + 512)
            P.op("vector", lambda e, cb=cb: e.tensor_scalar(
                out=murow[0:1, cb], in0=ps_r1[0:1, 0:512], scalar1=1.0 / D, scalar2=None,
                op0=ALU.mult), sv)
            P.op("vector", lambda e, cb=cb: e.tensor_scalar(
                out=lnt1[0:1, cb], in0=ps_r2[0:1, 0:512], scalar1=1.0 / D, scalar2=None,
                op0=ALU.mult), sv)
            P.wait("tensor", sv)
        P.wait("vector", sv)
        P.op("vector", lambda e: e.tensor_tensor(out=rhorow[:], in0=murow[:], in1=murow[:], op=ALU.mult), sv)
        P.wait("vector", sv)
        P.op("vector", lambda e: e.tensor_tensor(out=lnt1, in0=lnt1, in1=rhorow[:], op=ALU.subtract), sv)
        P.wait("vector", sv)
        P.wait("scalar", sv)
        P.op("scalar", lambda e: e.activation(out=lnt1, in_=lnt1, func=AF.Ln, bias=epsr[0:1, 0:1]), sa)
        P.wait("scalar", sa)
        P.op("scalar", lambda e: e.activation(out=rhorow[:], in_=lnt1, func=AF.Exp, scale=-0.5), sa)
        P.op("scalar", lambda e: e.activation(out=lnt1, in_=lnt1, func=AF.Exp, scale=0.5), sa)
        sfence()
        P.wait("vector", sa)
        P.op("vector", lambda e: e.tensor_copy(out=mub[:], in_=murow[:]), sv)
        P.op("vector", lambda e: e.tensor_copy(out=rhob[:], in_=rhorow[:]), sv)
        P.op("vector", lambda e: e.tensor_copy(out=isb_[:], in_=lnt1), sv)
        P.wait("tensor", sv)
        for nb in range(4):
            P.op("tensor", lambda e, nb=nb: e.matmul(
                out=ps_mm[:, 0:512], lhsT=onesrow[0:1, 0:128], rhs=rhob[0:1, nb * 512:nb * 512 + 512],
                start=True, stop=True), sp)
            P.wait("vector", sp)
            P.op("vector", lambda e, nb=nb: e.tensor_copy(
                out=rhobc[:, nb * 512:nb * 512 + 512], in_=ps_mm[:, 0:512]), sv)
            P.wait("tensor", sv)
        bar()

    def emit_qkv():
        for fi, (dst, scale) in enumerate(((qb, 1.0), (kb, 0.125), (vb, 1.0))):
            for nb in range(4):
                cb = slice(nb * 512, nb * 512 + 512)
                P.wait("tensor", sv)
                for dmc in range(8):
                    P.op("tensor", lambda e, fi=fi, dmc=dmc, nb=nb: e.matmul(
                        out=ps_mm[:, 0:512],
                        lhsT=wqb[:, dmc * 384 + fi * 128:dmc * 384 + fi * 128 + 128],
                        rhs=scr[dmc // 4][:, (dmc % 4) * 2048 + nb * 512:(dmc % 4) * 2048 + nb * 512 + 512],
                        start=(dmc == 0), stop=False), sp)
                P.op("tensor", lambda e, fi=fi, nb=nb: e.matmul(
                    out=ps_mm[:, 0:512], lhsT=rowsum_nb[0:1, fi * 128:fi * 128 + 128],
                    rhs=mub[0:1, nb * 512:nb * 512 + 512], start=False, stop=False), sp)
                P.op("tensor", lambda e, fi=fi, nb=nb: e.matmul(
                    out=ps_mm[:, 0:512], lhsT=bqkv_b[0:1, fi * 128:fi * 128 + 128],
                    rhs=isb_[0:1, nb * 512:nb * 512 + 512], start=False, stop=True), sp)
                P.wait("vector", sp)
                P.op("vector", lambda e, dst=dst, cb=cb, scale=scale: e.scalar_tensor_tensor(
                    out=dst[:, cb], in0=ps_mm[:, 0:512], scalar=scale, in1=rhobc[:, cb],
                    op0=ALU.mult, op1=ALU.mult), sv)
        bar()

    for b in range(NBLK):
        emit_stats(lambda dmc, hh, b=b: XT[dmc * 128:(dmc + 1) * 128,
                                           b * TB + hh * 1024:b * TB + hh * 1024 + 1024])
        emit_qkv()
        if debug:
            P.wait("gpsimd", sv)
            for srci, src in enumerate((qb, kb, vb)):
                P.op("gpsimd", lambda e, src=src, srci=srci, b=b: e.dma_start(
                    out=dbg["qkv"][srci * 128:srci * 128 + 128, b * TB:(b + 1) * TB], in_=src[:]), st_, 16)

        P.wait("tensor", sv)
        for s in range(C):
            for si, (src, dstP) in enumerate(((qb, qP), (kb, kP), (vb, vP))):
                trp = ps_tr if (s * 3 + si) % 2 == 0 else ps_trB
                P.op("tensor", lambda e, src=src, s=s, trp=trp: e.transpose(
                    out=trp[:], in_=view(src[:], s, [[C, 128]]), identity=identt[:]), sp)
                P.wait("vector", sp)
                P.op("vector", lambda e, dstP=dstP, s=s, trp=trp: e.tensor_copy(
                    out=dstP[:, s * 128:(s + 1) * 128], in_=trp[:, 0:128]), sv)
                P.wait("tensor", sv)
        bar()

        for nb in range(4):
            mrange = slice(nb * 512, nb * 512 + 512)
            P.wait("tensor", sv)
            P.op("tensor", lambda e, mrange=mrange: e.matmul(
                out=ps_y[:, 0:512], lhsT=wst_bd[:], rhs=qb[:, mrange], start=True, stop=False), sp)
            for n in range(C):
                P.op("vector", lambda e, nb=nb, n=n: e.tensor_tensor(
                    out=view(scr[0][:], n, [[16, 32], [1, 16 - n]]),
                    in0=view(qb[:], nb * 512 + n, [[16, 32], [1, 16 - n]]),
                    in1=view(kb[:], nb * 512, [[16, 32], [1, 16 - n]]), op=ALU.mult), sv)
                P.wait("tensor", sv)
                P.op("tensor", lambda e, n=n: e.matmul(
                    out=view(ps_z[:], n, [[16, 32], [1, 16 - n]]),
                    lhsT=pn_tab[n][:],
                    rhs=view(scr[0][:], n, [[16, 32], [1, 16 - n]]),
                    start=True, stop=True), sp)
                P.wait("vector", sp)
                P.op("vector", lambda e, nb=nb, n=n: e.scalar_tensor_tensor(
                    out=view(scr[1][:], n, [[16, 32], [1, 16 - n]]),
                    in0=view(ps_z[:], n, [[16, 32], [1, 16 - n]]),
                    scalar=1.0,
                    in1=view(vb[:], nb * 512, [[16, 32], [1, 16 - n]]),
                    op0=ALU.mult, op1=ALU.mult), sv)
                P.wait("tensor", sv)
                P.op("tensor", lambda e, n=n: e.matmul(
                    out=view(ps_y[:], n, [[16, 32], [1, 16 - n]]),
                    lhsT=identt[:],
                    rhs=view(scr[1][:], n, [[16, 32], [1, 16 - n]]),
                    start=False, stop=(n == C - 1)), sp)
            P.wait("vector", sp)
            P.op("vector", lambda e, mrange=mrange: e.tensor_copy(
                out=ysb[:, mrange], in_=ps_y[:, 0:512]), sv)
            P.wait("tensor", sv)
        bar()

        for s in range(C):
            vkdst = scr[1] if s == 0 else scr[0]
            for h in range(2):
                P.op("vector", lambda e, s=s, h=h, vkdst=vkdst: e.scalar_tensor_tensor(
                    out=view(vkdst[:], h * 4096, [[64, 64], [1, 64]]),
                    in0=view(vP[:], s * 128 + h * 64, [[1, 64], [0, 64]]),
                    scalar=1.0,
                    in1=view(kP[:], s * 128 + h * 64, [[0, 64], [1, 64]]),
                    op0=ALU.mult, op1=ALU.mult), sv)
            if s > 0:
                P.op("vector", lambda e: e.tensor_tensor(
                    out=scr[1][:], in0=scr[1][:], in1=rbc[:], op=ALU.mult), sv)
                P.op("vector", lambda e: e.tensor_tensor(
                    out=scr[1][:], in0=scr[1][:], in1=scr[0][:], op=ALU.add), sv)
        bar()

        for fs in range(64):
            trp = ps_tr if fs % 2 == 0 else ps_trB
            P.wait("tensor", sv)
            P.op("tensor", lambda e, fs=fs, trp=trp: e.transpose(
                out=trp[:], in_=scr[1][:, fs * 128:(fs + 1) * 128], identity=identt[:]), sp)
            P.wait("vector", sp)
            P.op("vector", lambda e, fs=fs: e.tensor_copy(
                out=sigext[:, 0:1], in_=carry[:, fs:fs + 1]), sv)
            P.op("vector", lambda e, fs=fs, trp=trp: e.tensor_tensor_scan(
                out=sigf[:], data0=rcmir[:, fs:fs + 1].broadcast_to([128, 128]),
                data1=trp[:, 0:128], initial=carry[:, fs:fs + 1],
                op0=ALU.mult, op1=ALU.add), sv)
            P.wait("vector", sv)
            P.op("vector", lambda e, fs=fs: e.tensor_copy(
                out=carry[:, fs:fs + 1], in_=sigf[:, 127:128]), sv)
            P.op("vector", lambda e: e.tensor_copy(out=sigext[:, 1:128], in_=sigf[:, 0:127]), sv)
            P.wait("tensor", sv)
            P.op("tensor", lambda e: e.transpose(
                out=ps_tr2[:], in_=sigext[:, 0:128], identity=identt[:]), sp)
            P.wait("vector", sp)
            P.op("vector", lambda e, fs=fs: e.tensor_copy(
                out=sigm[:, fs * 128:(fs + 1) * 128], in_=ps_tr2[:, 0:128]), sv)
        bar()
        if debug:
            P.wait("gpsimd", sv)
            P.op("gpsimd", lambda e, b=b: e.dma_start(
                out=dbg["sig"][b * 128:(b + 1) * 128, :], in_=sigm[:]), st_, 16)

        for c in range(C):
            if c == 0:
                P.op("vector", lambda e: e.tensor_tensor(
                    out=scr[1][:], in0=anc0[:], in1=sigm[:], op=ALU.mult), sv)
            else:
                P.op("vector", lambda e: e.tensor_tensor(
                    out=scr[1][:], in0=scr[1][:], in1=rbc[:], op=ALU.mult), sv)
            for h in range(2):
                P.op("vector", lambda e, c=c, h=h: e.tensor_tensor(
                    out=view(scr[0][:], h * 4096, [[64, 64], [1, 64]]),
                    in0=view(scr[1][:], h * 4096, [[64, 64], [1, 64]]),
                    in1=view(qP[:], c * 128 + h * 64, [[0, 64], [1, 64]]), op=ALU.mult), sv)
            for w in (32, 16, 8):
                P.op("vector", lambda e, w=w: e.tensor_tensor(
                    out=view(scr[0][:], 0, [[64, 128], [1, w]]),
                    in0=view(scr[0][:], 0, [[64, 128], [1, w]]),
                    in1=view(scr[0][:], w, [[64, 128], [1, w]]), op=ALU.add), sv)
            P.op("vector", lambda e, c=c: e.tensor_reduce(
                out=view(sigf[:], 0, [[1, 128], [1, 1]]),
                in_=view(scr[0][:], 0, [[64, 128], [1, 8]]),
                axis=AX.X, op=ALU.add), sv)
            P.wait("vector", sv)
            P.op("vector", lambda e, c=c: e.tensor_copy(
                out=z2s[:, c * 128:(c + 1) * 128], in_=sigf[:, 0:128]), sv)
        bar()

        for c in range(C):
            trp = ps_tr if c % 2 == 0 else ps_trB
            P.wait("tensor", sv)
            P.op("tensor", lambda e, c=c, trp=trp: e.transpose(
                out=trp[:], in_=z2s[:, c * 128:(c + 1) * 128], identity=identt[:]), sp)
            P.wait("vector", sp)
            P.op("vector", lambda e, c=c, trp=trp: e.tensor_tensor(
                out=view(ysb[:], c, [[16, 128]]),
                in0=view(ysb[:], c, [[16, 128]]),
                in1=trp[:, 0:128], op=ALU.add), sv)
        P.wait("vector", sl2)
        P.op("vector", lambda e: e.tensor_copy(out=z2s[:, 0:2048], in_=ysb[:]), sv)
        vfence()
        P.wait("gpsimd", sv)
        P.op("gpsimd", lambda e, b=b: e.dma_start(
            out=YSLICE[:, b * TB:(b + 1) * TB], in_=z2s[:, 0:2048]), sl2, 16)
        if debug:
            P.op("gpsimd", lambda e, b=b: e.dma_start(
                out=dbg["y"][:, b * TB:(b + 1) * TB], in_=z2s[:, 0:2048]), st_, 16)
        bar()

    P.wait("gpsimd", sl2)
    P.op("gpsimd", lambda e: e.collective_compute(
        "AllGather", ALU.bypass, replica_groups=[list(range(NCORE))],
        ins=[YSLICE.opt()], outs=[YGATH.opt()]), scc, 1)
    P.mark(scc)

    esA.close()
    esP.close()
    esB = ExitStack()
    sbB = mk(esB)
    ps_mm2 = psum("ps_mm2", [128, 352], F32, esB)
    ps_mmg = psum("ps_mmg", [128, 352], F32, esB)
    mubc = sbB("mubc", [128, TB], F32)
    rhobcf = sbB("rhobcf", [128, TB], F32)
    xnf = sbB("xnf", [128, TB], F32)
    xtokf = sbB("xtokf", [128, 128], F32)
    qtbB = sbB("qtbB", [128, 1024], BF16)
    qsgB = sbB("qsgB", [128, 1024], F32)
    hq = sbB("hq", [128, 24 * 256], BF16)
    sgt = sbB("sgt", [128, 352], BF16)
    htokf = sbB("htokf", [128, 352], F32)
    hqtok = sbB("hqtok", [128, 352], BF16)
    ygt = sbB("ygt", [128, 8 * 256], BF16)
    wgt = sbB("wgt", [128, 8 * 352], BF16)
    wut = sbB("wut", [128, 8 * 352], BF16)
    wdt = [sbB(f"wdt{i}", [128, 128], BF16) for i in range(2)]
    wot = [sbB(f"wot{i}", [128, 128], BF16) for i in range(2)]
    obuf = sbB("obuf", [128, 512], F32)
    xrs = sbB("xrs", [128, 512], F32)

    emit_stats(lambda dmc, hh: XTO[dmc * 128:(dmc + 1) * 128, hh * 1024:hh * 1024 + 1024])
    P.wait("gpsimd", sv)
    P.op("gpsimd", lambda e: e.dma_start(out=MRHO[0:1, :], in_=murow[:]), sl2, 16)
    P.op("gpsimd", lambda e: e.dma_start(out=MRHO[1:2, :], in_=rhorow[:]), sl2, 16)
    P.wait("gpsimd", sl2)
    P.op("gpsimd", lambda e: e.dma_start(out=mubc[:], in_=MRHO[0:1, :].broadcast_to([128, TB])), sl2, 16)
    P.op("gpsimd", lambda e: e.dma_start(out=rhobcf[:], in_=MRHO[1:2, :].broadcast_to([128, TB])), sl2, 16)
    P.wait("vector", sl2)
    bar()

    for dmc in range(8):
        for hh in range(2):
            P.wait("sync", sv)
            P.op("sync", lambda e, dmc=dmc, hh=hh: e.dma_start(
                out=xf[hh % 2][:], in_=XTO[dmc * 128:(dmc + 1) * 128, hh * 1024:hh * 1024 + 1024]), sl, 16)
            P.wait("vector", sl)
            P.op("vector", lambda e, dmc=dmc, hh=hh: e.tensor_tensor(
                out=xnf[:, hh * 1024:hh * 1024 + 1024], in0=xf[hh % 2][:],
                in1=mubc[:, hh * 1024:hh * 1024 + 1024], op=ALU.subtract), sv)
        P.op("vector", lambda e: e.tensor_tensor(
            out=xnf[:], in0=xnf[:], in1=rhobcf[:], op=ALU.mult), sv)
        for tc in range(16):
            P.wait("tensor", sv)
            P.op("tensor", lambda e, tc=tc: e.matmul(
                out=ps_mm[:, 0:128], lhsT=xnf[:, tc * 128:(tc + 1) * 128],
                rhs=identf[:, 0:128], start=True, stop=True), sp)
            P.wait("vector", sp)
            P.op("vector", lambda e: e.tensor_copy(out=xtokf[:], in_=ps_mm[:, 0:128]), sv)
            quant_tile(xtokf[:, 0:128], qtbB[:, 128:256], 128, 128, qtbB, qsgB)
            P.wait("tensor", sv)
            P.op("tensor", lambda e: e.transpose(
                out=ps_tr[:], in_=qtbB[:, 128:256], identity=identt[:]), sp)
            P.wait("vector", sp)
            P.op("vector", lambda e, dmc=dmc, tc=tc: e.tensor_copy(
                out=scr[dmc // 4][:, (dmc % 4) * 2048 + tc * 128:(dmc % 4) * 2048 + tc * 128 + 128],
                in_=ps_tr[:, 0:128]), sv)
            P.wait("tensor", sv)
        bar()
    if debug:
        P.wait("gpsimd", sv)
        for dmc in range(8):
            P.op("gpsimd", lambda e, dmc=dmc: e.dma_start(
                out=dbg["xq"][dmc * 128:(dmc + 1) * 128, :],
                in_=scr[dmc // 4][:, (dmc % 4) * 2048:(dmc % 4) * 2048 + 2048]), st_, 16)

    pid = nc.partition_id([mybir.EngineType.SP])
    P.wait("gpsimd", scc)
    P.wait("sync", scc)
    if debug:
        P.op("gpsimd", lambda e: e.dma_start(out=dbg["wdq"][:, :], in_=WDQ_SL[0:IFF, :]), st_, 16)
        P.wait("gpsimd", st_)
    NS = 8
    TS = 256
    for sub in range(NS):
        P.wait("sync", sp)
        P.op("sync", lambda e, sub=sub: e.dma_start(
            out=ygt[:].rearrange("p (r n) -> p r n", r=8),
            in_=dataclasses.replace(
                YGATH, offset=YGATH.offset + pid * TB + sub * TS,
                ap=[[L, 128], [128 * L, 8], [1, TS]])), sl, 16)
        if debug and sub == 0:
            P.wait("gpsimd", sl)
            P.op("gpsimd", lambda e: e.dma_start(out=dbg["ygt"][:, :], in_=ygt[:]), st_, 16)
            P.wait("sync", st_)
        for r in range(8):
            P.wait("sync", sp)
            P.op("sync", lambda e, r=r: e.dma_start(
                out=wgt[:].rearrange("p (c n) -> p c n", c=8),
                in_=WGQ[r * 1024:(r + 1) * 1024, :].rearrange("(c p) n -> p c n", p=128)), sl, 16)
            P.op("sync", lambda e, r=r: e.dma_start(
                out=wut[:].rearrange("p (c n) -> p c n", c=8),
                in_=WUQ[r * 1024:(r + 1) * 1024, :].rearrange("(c p) n -> p c n", p=128)), sl, 16)
            for tg in range(2):
                P.wait("tensor", sl)
                P.wait("tensor", sv)
                P.wait("tensor", sa)
                for dmc in range(8):
                    xoff = (dmc % 4) * 2048 + sub * TS + tg * 128
                    P.op("tensor", lambda e, dmc=dmc, xoff=xoff: e.matmul(
                        out=ps_mmg[:, 0:352], lhsT=scr[dmc // 4][:, xoff:xoff + 128],
                        rhs=wgt[:, dmc * 352:(dmc + 1) * 352],
                        start=(dmc == 0), stop=(dmc == 7)), sp)
                for dmc in range(8):
                    xoff = (dmc % 4) * 2048 + sub * TS + tg * 128
                    P.op("tensor", lambda e, dmc=dmc, xoff=xoff: e.matmul(
                        out=ps_mm2[:, 0:352], lhsT=scr[dmc // 4][:, xoff:xoff + 128],
                        rhs=wut[:, dmc * 352:(dmc + 1) * 352],
                        start=(dmc == 0), stop=(dmc == 7)), sp)
                P.wait("scalar", sp)
                P.op("scalar", lambda e: e.activation(out=sgt[:, 0:352], in_=ps_mmg[:, 0:352],
                                                      func=AF.Silu), sa)
                P.wait("vector", sa)
                P.op("vector", lambda e: e.scalar_tensor_tensor(
                    out=htokf[:, 0:352], in0=ps_mm2[:, 0:352], scalar=1.0, in1=sgt[:, 0:352],
                    op0=ALU.mult, op1=ALU.mult), sv)
                quant_tile(htokf[:, 0:352], hqtok[:, 0:352], 128, 352, qtbB, qsgB)
                P.wait("tensor", sv)
                for kk, kw in ((0, 128), (1, 128), (2, 96)):
                    trp = ps_tr if kk % 2 == 0 else ps_tr2
                    P.op("tensor", lambda e, kk=kk, kw=kw, trp=trp: e.transpose(
                        out=trp[0:kw, 0:128], in_=hqtok[:, kk * 128:kk * 128 + kw],
                        identity=identt[:]), sp)
                    P.wait("vector", sp)
                    slot = r * 3 + kk
                    P.op("vector", lambda e, slot=slot, tg=tg, kw=kw, trp=trp: e.tensor_copy(
                        out=hq[0:kw, slot * TS + tg * 128:slot * TS + tg * 128 + 128],
                        in_=trp[0:kw, 0:128]), sv)
                    P.wait("tensor", sv)
            bar()
        if debug and sub == 0:
            P.wait("gpsimd", sv)
            P.op("gpsimd", lambda e: e.dma_start(out=dbg["hq"][:, :], in_=hq[:]), st_, 16)

        for fr in range(8):
            P.wait("tensor", sv)
            first = True
            mm_marks = []
            for r in (range(8) if PARTS["ffn"] else ()):
                for kk, kw in ((0, 128), (1, 128), (2, 96)):
                    slot = r * 3 + kk
                    wb = wdt[slot % 2]
                    if len(mm_marks) >= 2:
                        P.wait("sync", sp, mm_marks[-2])
                    P.op("sync", lambda e, fr=fr, r=r, kk=kk, kw=kw, wb=wb: e.dma_start(
                        out=wb[0:kw, 0:128],
                        in_=WDQ[fr * IFF + r * 352 + kk * 128:
                                fr * IFF + r * 352 + kk * 128 + kw, :]), sl, 16)
                    P.wait("tensor", sl)
                    P.op("tensor", lambda e, slot=slot, kw=kw, wb=wb, first=first: e.matmul(
                        out=ps_y[:, 0:TS], lhsT=wb[0:kw, 0:128],
                        rhs=hq[0:kw, slot * TS:(slot + 1) * TS],
                        start=first, stop=False), sp)
                    mm_marks.append(P.mark(sp))
                    first = False
            for r in (range(8) if PARTS["attn"] else ()):
                wo = wot[r % 2]
                if r >= 2:
                    P.wait("sync", sp, mm_marks[-2])
                P.op("sync", lambda e, r=r, fr=fr, wo=wo: e.dma_start(
                    out=wo[:, 0:128],
                    in_=OWB16[r * 128:(r + 1) * 128, fr * 128:(fr + 1) * 128]), sl3, 16)
                P.wait("tensor", sl3)
                P.op("tensor", lambda e, r=r, wo=wo, first=first: e.matmul(
                    out=ps_y[:, 0:TS], lhsT=wo[:, 0:128],
                    rhs=ygt[:, r * TS:(r + 1) * TS], start=first, stop=False), sp)
                first = False
                mm_marks.append(P.mark(sp))
            if PARTS["bias"]:
                P.op("tensor", lambda e, fr=fr, first=first: e.matmul(
                    out=ps_y[:, 0:TS], lhsT=outbb[0:1, fr * 128:fr * 128 + 128],
                    rhs=onesrow[0:1, 0:TS], start=first, stop=True), sp)
                first = False
            else:
                P.op("tensor", lambda e, fr=fr: e.matmul(
                    out=ps_y[:, 0:TS], lhsT=outbb[0:1, fr * 128:fr * 128 + 128],
                    rhs=onesrow[0:1, 0:TS], start=False, stop=True), sp) if False else None
            P.op("sync", lambda e, fr=fr, sub=sub: e.dma_start(
                out=xrs[:, 0:TS],
                in_=XTO[fr * 128:(fr + 1) * 128, sub * TS:sub * TS + TS]), sl3, 16)
            P.wait("vector", sp)
            P.wait("vector", sl3)
            P.wait("vector", st_)
            P.op("vector", lambda e: e.scalar_tensor_tensor(
                out=obuf[:, 0:TS], in0=ps_y[:, 0:TS], scalar=1.0, in1=xrs[:, 0:TS],
                op0=ALU.mult, op1=ALU.add), sv)
            vfence()
            P.wait("gpsimd", sv)
            P.op("gpsimd", lambda e, fr=fr, sub=sub: e.dma_start(
                out=OUT[fr * 128:(fr + 1) * 128, sub * TS:sub * TS + TS],
                in_=obuf[:, 0:TS]), st_, 16)
            P.wait("sync", sv)
            bar()

    P.wait("gpsimd", st_)
    P.wait("sync", st_)
    esB.close()

    with nc.Block() as block:
        @block.sync
        def _(e):
            P.replay("sync", e)

        @block.vector
        def _(e):
            P.replay("vector", e)

        @block.scalar
        def _(e):
            P.replay("scalar", e)

        @block.tensor
        def _(e):
            P.replay("tensor", e)

        @block.gpsimd
        def _(e):
            P.replay("gpsimd", e)

    es.close()
    return nc


_prog_cache = {}


def get_program(debug=False):
    if debug not in _prog_cache:
        _prog_cache[debug] = build_program(debug)
    return _prog_cache[debug]


def make_inmaps(inputs):
    x = np.asarray(inputs["L0_attn_x"])[0]
    xT = np.ascontiguousarray(x.T)
    wqkv = np.asarray(inputs["L0_attn_Wqkv_w"])
    bqkv = np.asarray(inputs["L0_attn_Wqkv_b"])
    outw = np.asarray(inputs["L0_attn_out_w"])
    outb = np.asarray(inputs["L0_attn_out_b"])
    wst = np.asarray(inputs["L0_attn_W_static"])
    lam = np.asarray(inputs["L0_attn_Lambda_raw"])
    gam = np.asarray(inputs["L0_attn_Gamma"])
    wg = np.asarray(inputs["L0_ffn_w_gate"])
    wu = np.asarray(inputs["L0_ffn_w_up"])
    wd = np.asarray(inputs["L0_ffn_w_down"])
    ident = np.eye(128, dtype=np.float32)
    wqkvT = np.ascontiguousarray(wqkv.T)
    outwT = np.ascontiguousarray(outw.T)

    maps = []
    for k in range(NCORE):
        lam_ij = np.ascontiguousarray(lam[2 * k:2 * k + 2].reshape(128, 64))
        gam_ij = np.ascontiguousarray(gam[2 * k:2 * k + 2].reshape(128, 64))
        lam_ji = np.ascontiguousarray(np.transpose(lam[2 * k:2 * k + 2], (0, 2, 1)).reshape(128, 64))
        gam_ji = np.ascontiguousarray(np.transpose(gam[2 * k:2 * k + 2], (0, 2, 1)).reshape(128, 64))
        wst_ji = np.ascontiguousarray(np.transpose(wst[2 * k:2 * k + 2], (0, 2, 1)).reshape(128, 64))
        cols = np.concatenate([np.arange(2 * k * 64, 2 * k * 64 + 128),
                               1024 + np.arange(2 * k * 64, 2 * k * 64 + 128),
                               2048 + np.arange(2 * k * 64, 2 * k * 64 + 128)])
        maps.append({
            "xt": xT,
            "xto": np.ascontiguousarray(xT[:, k * TB:(k + 1) * TB]),
            "wqkvt": np.ascontiguousarray(wqkvT[:, cols]),
            "bqkv": np.ascontiguousarray(bqkv[cols]).reshape(1, 384),
            "outwt": outwT,
            "outb": outb.reshape(1, D),
            "lam_ij": lam_ij, "gam_ij": gam_ij,
            "lam_ji": lam_ji, "gam_ji": gam_ji,
            "wstat_ji": wst_ji,
            "wg_sl": np.ascontiguousarray(wg[k * 352:(k + 1) * 352, :]),
            "wu_sl": np.ascontiguousarray(wu[k * 352:(k + 1) * 352, :]),
            "wd_sl": np.ascontiguousarray(wd[k * 128:(k + 1) * 128, :]),
            "ident": ident,
        })
    return maps


def kernel(**inputs):
    nc = get_program(debug=False)
    maps = make_inmaps(inputs)
    res = run_bass_kernel_spmd(nc, maps, core_ids=list(range(NCORE)))
    outs = [np.asarray(res.results[k]["outx"], np.float32).T for k in range(NCORE)]
    return np.concatenate(outs, axis=0)[None]


# version 06
# speedup vs baseline: 1.0929x; 1.0929x over previous
import dataclasses
from contextlib import ExitStack

import numpy as np
import ml_dtypes

import concourse.bass as bass
import concourse.mybir as mybir
from concourse.bass_utils import run_bass_kernel_spmd

F32 = mybir.dt.float32
BF16 = mybir.dt.bfloat16
I32 = mybir.dt.int32
AF = mybir.ActivationFunctionType
ALU = mybir.AluOpType
AX = mybir.AxisListType
bfloat16 = ml_dtypes.bfloat16

L = 16384
D = 1024
NCORE = 8
IFF = 2816
C = 16
TB = 2048
NBLK = L // TB
EXP_MASK = 0x7F800000
SIGN_MASK = int(np.uint32(0x80000000).astype(np.int32))
ENGS = ("sync", "vector", "scalar", "tensor", "gpsimd")


class Sem:
    def __init__(self, h):
        self.h = h
        self.n = 0


class Plan:

    def __init__(self):
        self.ops = {e: [] for e in ENGS}
        self.pending = {}

    def _flush(self, sem):
        p = self.pending.get(sem)
        if p is not None:
            eng, cnt, rec = p
            sem.n += cnt
            rec[1] = (sem, cnt)
            self.pending[sem] = None

    def wait(self, eng, sem, val=None):
        if val is None:
            self._flush(sem)
            v = sem.n
        else:
            v = val
            if v > sem.n:
                self._flush(sem)
                v = min(v, sem.n)
        if v > 0:
            self.ops[eng].append([lambda e, s=sem, v=v: e.wait_ge(s.h, v), None, True])

    def op(self, eng, fn, sem=None, k=1):
        rec = [fn, None, False]
        self.ops[eng].append(rec)
        if sem is not None:
            if k == 16:
                sem.n += k
                rec[1] = (sem, k)
            else:
                p = self.pending.get(sem)
                if p is not None and p[0] == eng:
                    self.pending[sem] = (eng, p[1] + k, rec)
                else:
                    self._flush(sem)
                    self.pending[sem] = (eng, k, rec)

    def mark(self, sem):
        self._flush(sem)
        return sem.n

    def replay(self, eng, e):
        for rec in self.ops[eng]:
            fn, inc, is_wait = rec
            if is_wait:
                fn(e)
            else:
                instr = fn(e)
                if inc is not None:
                    instr.then_inc(inc[0].h, inc[1])


def view(ap2d, extra_off, dims):
    a = ap2d.copy()
    newap = [list(a.ap[0])] + [list(x) for x in dims]
    return dataclasses.replace(a, offset=a.offset + extra_off, ap=newap)


PARTS = {"attn": True, "ffn": True, "bias": True}


def build_program(debug=False):
    nc = bass.Bass("TRN2", target_bir_lowering=False)
    es = ExitStack()

    def mk(stack):
        def sb(name, shape, dtype):
            return stack.enter_context(nc.sbuf_tensor(name, shape, dtype))
        return sb

    sb = mk(es)

    def sem(name):
        return Sem(es.enter_context(nc.semaphore(name)))

    XT = nc.dram_tensor("xt", [D, L], F32, kind="ExternalInput").ap()
    XTO = nc.dram_tensor("xto", [D, TB], F32, kind="ExternalInput").ap()
    WQKVT = nc.dram_tensor("wqkvt", [D, 384], F32, kind="ExternalInput").ap()
    BQKV = nc.dram_tensor("bqkv", [1, 384], F32, kind="ExternalInput").ap()
    OUTWT = nc.dram_tensor("outwt", [D, D], F32, kind="ExternalInput").ap()
    OUTB = nc.dram_tensor("outb", [1, D], F32, kind="ExternalInput").ap()
    LAM_IJ = nc.dram_tensor("lam_ij", [128, 64], F32, kind="ExternalInput").ap()
    GAM_IJ = nc.dram_tensor("gam_ij", [128, 64], F32, kind="ExternalInput").ap()
    LAM_JI = nc.dram_tensor("lam_ji", [128, 64], F32, kind="ExternalInput").ap()
    GAM_JI = nc.dram_tensor("gam_ji", [128, 64], F32, kind="ExternalInput").ap()
    WSTAT_JI = nc.dram_tensor("wstat_ji", [128, 64], F32, kind="ExternalInput").ap()
    WG_SL = nc.dram_tensor("wg_sl", [352, D], F32, kind="ExternalInput").ap()
    WU_SL = nc.dram_tensor("wu_sl", [352, D], F32, kind="ExternalInput").ap()
    WD_SL = nc.dram_tensor("wd_sl", [128, IFF], F32, kind="ExternalInput").ap()
    IDENT = nc.dram_tensor("ident", [128, 128], F32, kind="ExternalInput").ap()
    OUT = nc.dram_tensor("outx", [D, TB], F32, kind="ExternalOutput").ap()

    YSLICE = nc.dram_tensor("yslice", [128, L], BF16).ap()
    YGATH = nc.dram_tensor("ygath", [NCORE * 128, L], BF16).ap()
    WGQ_SL = nc.dram_tensor("wgq_sl", [D, 352], BF16).ap()
    WUQ_SL = nc.dram_tensor("wuq_sl", [D, 352], BF16).ap()
    WDQ_SL = nc.dram_tensor("wdq_sl", [IFF, 128], BF16).ap()
    WGQ = nc.dram_tensor("wgq", [NCORE * D, 352], BF16).ap()
    WUQ = nc.dram_tensor("wuq", [NCORE * D, 352], BF16).ap()
    WDQ = nc.dram_tensor("wdq", [NCORE * IFF, 128], BF16).ap()
    RFLAT = nc.dram_tensor("rflat", [4, 8192], F32).ap()
    MRHO = nc.dram_tensor("mrho", [2, TB], F32).ap()
    OWB16 = nc.dram_tensor("owb16", [D, D], BF16).ap()
    dbg = {}
    if debug:
        dbg["qkv"] = nc.dram_tensor("dbg_qkv", [384, L], BF16, kind="ExternalOutput").ap()
        dbg["sig"] = nc.dram_tensor("dbg_sig", [NBLK * 128, 8192], BF16, kind="ExternalOutput").ap()
        dbg["y"] = nc.dram_tensor("dbg_y", [128, L], BF16, kind="ExternalOutput").ap()
        dbg["xq"] = nc.dram_tensor("dbg_xq", [D, TB], BF16, kind="ExternalOutput").ap()
        dbg["hq"] = nc.dram_tensor("dbg_hq", [128, 24 * 256], BF16, kind="ExternalOutput").ap()
        dbg["ygt"] = nc.dram_tensor("dbg_ygt", [128, 8 * 256], BF16, kind="ExternalOutput").ap()
        dbg["wdq"] = nc.dram_tensor("dbg_wdq", [IFF, 128], BF16, kind="ExternalOutput").ap()

    scr = [sb(f"scr{i}", [128, 8192], BF16) for i in range(2)]
    xf = [sb(f"xf{i}", [128, 1024], F32) for i in range(2)]
    qsf = sb("qsf", [128, 2048], F32)
    wqb = sb("wqb", [128, 8 * 384], BF16)
    identt = sb("identt", [128, 128], BF16)
    identf = sb("identf", [128, 128], F32)
    onesrow = sb("onesrow", [1, 512], BF16)
    onescol = sb("onescol", [128, 1], BF16)
    rowsum = sb("rowsum", [1, 384], F32)
    rowsum_nb = sb("rowsum_nb", [1, 384], BF16)
    bqkv_b = sb("bqkv_b", [1, 384], BF16)
    murow = sb("murow", [1, TB], F32)
    mub = sb("mub", [1, TB], BF16)
    rhorow = sb("rhorow", [1, TB], F32)
    rhob = sb("rhob", [1, TB], BF16)
    isb_ = sb("isb_", [1, TB], BF16)
    rhobc = sb("rhobc", [128, TB], BF16)
    qs1 = sb("qs1", [128, 128], F32)
    qsc = sb("qsc", [128, 128], F32)
    outbb = sb("outbb", [1, D], BF16)
    sq5 = [sb(f"sq5{i}", [128, 512], BF16) for i in range(2)]
    epsr = sb("epsr", [1, 1], F32)

    sl = sem("sl")
    sl2 = sem("sl2")
    sl3 = sem("sl3")
    sv = sem("sv")
    sa = sem("sa")
    sp = sem("sp")
    st_ = sem("st")
    scc = sem("scc")

    def psum(name, shape, dtype=F32, stack=None):
        return (stack or es).enter_context(nc.psum_tensor(name, shape, dtype))

    ps_r1 = psum("ps_r1", [1, 512], F32)
    ps_r2 = psum("ps_r2", [1, 512], F32)
    ps_mm = psum("ps_mm", [128, 512], F32)
    ps_y = psum("ps_y", [128, 512], F32)
    ps_tr = psum("ps_tr", [128, 128], BF16)
    ps_tr2 = psum("ps_tr2", [128, 128], BF16)

    P = Plan()

    def bar():
        for _e in ENGS:
            for _s in (sl, sl2, sl3, sv, sa, sp, st_):
                P.wait(_e, _s)

    lnt1 = qsf[0:1, 0:2048]
    QXN = qsf[:, 0:1024]
    QB_ = qsf[:, 1024:2048]

    esP = ExitStack()
    sbP = mk(esP)
    r_ij = sbP("r_ij", [128, 64], F32)
    rc_ij = sbP("rc_ij", [128, 64], F32)
    gr_ij = sbP("gr_ij", [128, 64], F32)
    r_ji = sbP("r_ji", [128, 64], F32)
    gam_ji = sbP("gam_ji_s", [128, 64], F32)
    wst_ji = sbP("wst_ji_s", [128, 64], F32)
    ptmp = sbP("ptmp", [128, 64], F32)
    rcmir_in = sbP("rcmir_in", [64, 128], F32)
    qtbP = sbP("qtbP", [128, 1024], BF16)
    qsgP = sbP("qsgP", [128, 1024], F32)
    wqtmp = sbP("wqtmp", [128, 1024], BF16)

    P.op("sync", lambda e: e.dma_start(out=identf[:], in_=IDENT), sl, 16)
    P.op("sync", lambda e: e.dma_start(out=r_ij[:], in_=LAM_IJ), sl, 16)
    P.op("sync", lambda e: e.dma_start(out=gr_ij[:], in_=GAM_IJ), sl, 16)
    P.op("sync", lambda e: e.dma_start(out=r_ji[:], in_=LAM_JI), sl, 16)
    P.op("sync", lambda e: e.dma_start(out=gam_ji[:], in_=GAM_JI), sl, 16)
    P.op("sync", lambda e: e.dma_start(out=wst_ji[:], in_=WSTAT_JI), sl, 16)
    P.op("sync", lambda e: e.dma_start(out=rowsum[:], in_=BQKV), sl, 16)
    P.op("sync", lambda e: e.dma_start(out=lnt1[0:1, 0:1024], in_=OUTB), sl, 16)
    P.wait("scalar", sl)
    P.op("scalar", lambda e: e.activation(out=r_ij[:], in_=r_ij[:], func=AF.Sigmoid), sa)
    P.op("scalar", lambda e: e.activation(out=r_ji[:], in_=r_ji[:], func=AF.Sigmoid), sa)
    P.wait("vector", sa)
    P.op("vector", lambda e: e.tensor_scalar(out=r_ij[:], in0=r_ij[:], scalar1=-1.0, scalar2=1.0,
                                             op0=ALU.mult, op1=ALU.add), sv)
    P.op("vector", lambda e: e.tensor_scalar(out=r_ji[:], in0=r_ji[:], scalar1=-1.0, scalar2=1.0,
                                             op0=ALU.mult, op1=ALU.add), sv)
    P.op("vector", lambda e: e.tensor_copy(out=identt[:], in_=identf[:]), sv)
    P.op("vector", lambda e: e.tensor_copy(out=bqkv_b[:], in_=rowsum[:]), sv)
    P.op("vector", lambda e: e.tensor_copy(out=outbb[:], in_=lnt1[0:1, 0:1024]), sv)
    P.op("vector", lambda e: e.memset(epsr[:], 1e-5), sv)
    P.op("vector", lambda e: e.memset(onesrow[:], 1.0), sv)
    P.op("vector", lambda e: e.memset(onescol[:], 1.0), sv)

    esA = ExitStack()
    sbA = mk(esA)
    qb = sbA("qb", [128, TB], BF16)
    kb = sbA("kb", [128, TB], BF16)
    vb = sbA("vb", [128, TB], BF16)
    qP = sbA("qP", [128, C * 128], BF16)
    kP = sbA("kP", [128, C * 128], BF16)
    vP = sbA("vP", [128, C * 128], BF16)
    rbc = sbA("rbc", [128, 8192], BF16)
    anc0 = sbA("anc0", [128, 8192], BF16)
    sigm = sbA("sigm", [128, 8192], BF16)
    z2s = sbA("z2s", [128, C * 128], BF16)
    ysb = sbA("ysb", [128, TB], F32)
    pn_tab = [sbA(f"pn{n}", [128, 128], BF16) for n in range(C)]
    wst_bd = sbA("wst_bd", [128, 128], BF16)
    rcmir = sbA("rcmir", [128, 64], F32)
    carry = sbA("carry", [128, 64], F32)
    sigext = sbA("sigext", [128, 136], BF16)
    sigf = sbA("sigf", [128, 128], F32)
    ps_z = psum("ps_z", [128, 512], F32, esA)
    ps_trB = psum("ps_trB", [128, 128], BF16, esA)

    P.op("vector", lambda e: e.memset(carry[:], 0.0), sv)
    P.op("vector", lambda e: e.tensor_copy(out=ptmp[:], in_=gam_ji[:]), sv)
    for n in range(C):
        P.op("vector", lambda e, n=n: e.memset(pn_tab[n][:], 0.0), sv)
        P.op("vector", lambda e, n=n: e.tensor_copy(out=pn_tab[n][0:64, 0:64], in_=ptmp[0:64, :]), sv)
        P.op("vector", lambda e, n=n: e.tensor_copy(out=pn_tab[n][64:128, 64:128], in_=ptmp[64:128, :]), sv)
        if n + 1 < C:
            P.op("vector", lambda e: e.tensor_tensor(out=ptmp[:], in0=ptmp[:], in1=r_ji[:], op=ALU.mult), sv)
    P.op("vector", lambda e: e.memset(wst_bd[:], 0.0), sv)
    P.op("vector", lambda e: e.tensor_copy(out=wst_bd[0:64, 0:64], in_=wst_ji[0:64, :]), sv)
    P.op("vector", lambda e: e.tensor_copy(out=wst_bd[64:128, 64:128], in_=wst_ji[64:128, :]), sv)
    P.op("vector", lambda e: e.tensor_tensor(out=rc_ij[:], in0=r_ij[:], in1=r_ij[:], op=ALU.mult), sv)
    for _ in range(3):
        P.op("vector", lambda e: e.tensor_tensor(out=rc_ij[:], in0=rc_ij[:], in1=rc_ij[:], op=ALU.mult), sv)
    P.op("vector", lambda e: e.tensor_tensor(out=gr_ij[:], in0=gr_ij[:], in1=r_ij[:], op=ALU.mult), sv)
    P.wait("gpsimd", sv)
    P.op("gpsimd", lambda e: e.dma_start(
        out=RFLAT[0:1, :].rearrange("one (p f) -> (one p) f", p=128), in_=r_ij[:]), sl2, 16)
    P.op("gpsimd", lambda e: e.dma_start(
        out=RFLAT[1:2, :].rearrange("one (p f) -> (one p) f", p=128), in_=gr_ij[:]), sl2, 16)
    P.op("gpsimd", lambda e: e.dma_start(
        out=RFLAT[3:4, :].rearrange("one (p f) -> (one p) f", p=128), in_=rc_ij[:]), sl2, 16)
    P.wait("gpsimd", sl2)
    P.op("gpsimd", lambda e: e.dma_start(out=rbc[:], in_=RFLAT[0:1, :].broadcast_to([128, 8192])), sl2, 16)
    P.op("gpsimd", lambda e: e.dma_start(out=anc0[:], in_=RFLAT[1:2, :].broadcast_to([128, 8192])), sl2, 16)
    P.op("gpsimd", lambda e: e.dma_start(
        out=rcmir_in[:], in_=RFLAT[3:4, :].rearrange("one (a b) -> (one a) b", b=128)), sl2, 16)
    P.wait("tensor", sl2)
    P.wait("tensor", sv)
    P.op("tensor", lambda e: e.matmul(out=ps_mm[0:128, 0:64], lhsT=rcmir_in[:, 0:128],
                                      rhs=identf[0:64, 0:64], start=True, stop=True), sp)
    P.wait("vector", sp)
    P.wait("vector", sl2)
    P.op("vector", lambda e: e.tensor_copy(out=rcmir[:], in_=ps_mm[0:128, 0:64]), sv)

    for dmc in range(8):
        P.wait("sync", sv)
        P.op("sync", lambda e, dmc=dmc: e.dma_start(
            out=xf[0][:, 0:384], in_=WQKVT[dmc * 128:(dmc + 1) * 128, :]), sl, 16)
        P.wait("vector", sl)
        P.op("vector", lambda e, dmc=dmc: e.tensor_copy(
            out=wqb[:, dmc * 384:(dmc + 1) * 384], in_=xf[0][:, 0:384]), sv)
    for dmc in range(8):
        P.wait("sync", sv)
        P.op("sync", lambda e, dmc=dmc: e.dma_start(
            out=xf[0][:], in_=OUTWT[dmc * 128:(dmc + 1) * 128, :]), sl, 16)
        P.wait("vector", sl)
        P.op("vector", lambda e: e.tensor_copy(out=qtbP[:], in_=xf[0][:]), sv)
        P.wait("gpsimd", sv)
        P.op("gpsimd", lambda e, dmc=dmc: e.dma_start(
            out=OWB16[dmc * 128:(dmc + 1) * 128, :], in_=qtbP[:]), sl2, 16)
        P.wait("vector", sl2)
    P.wait("tensor", sv)
    for dmc in range(8):
        P.op("tensor", lambda e, dmc=dmc: e.matmul(
            out=ps_r1[0:1, 0:384], lhsT=onescol[:], rhs=wqb[:, dmc * 384:(dmc + 1) * 384],
            start=(dmc == 0), stop=(dmc == 7)), sp)
    P.wait("vector", sp)
    P.op("vector", lambda e: e.tensor_copy(out=rowsum[:], in_=ps_r1[0:1, 0:384]), sv)
    P.op("vector", lambda e: e.tensor_scalar(out=rowsum_nb[:], in0=rowsum[:], scalar1=-1.0,
                                             scalar2=None, op0=ALU.mult), sv)

    def quant_tile(src, dst, pcnt, ncols, qtb, qsg):
        nb16 = ncols // 16
        P.op("vector", lambda e: e.tensor_reduce(
            out=view(qs1[:], 0, [[1, nb16], [1, 1]])[0:pcnt],
            in_=view(src, 0, [[16, nb16], [1, 16]])[0:pcnt],
            axis=AX.X, op=ALU.max, apply_absolute_value=True), sv)
        P.wait("vector", sv)
        P.op("vector", lambda e: e.tensor_scalar(
            out=qs1[0:pcnt, 0:nb16], in0=qs1[0:pcnt, 0:nb16],
            scalar1=1.0 / 6.0, scalar2=1e-12, op0=ALU.mult, op1=ALU.max), sv)
        P.wait("vector", sv)
        P.op("vector", lambda e: e.reciprocal(out=qsc[0:pcnt, 0:nb16], in_=qs1[0:pcnt, 0:nb16]), sv)
        P.wait("vector", sv)
        P.op("vector", lambda e: e.tensor_tensor(
            out=view(QXN, 0, [[16, nb16], [1, 16]])[0:pcnt],
            in0=view(src, 0, [[16, nb16], [1, 16]])[0:pcnt],
            in1=view(qsc[:], 0, [[1, nb16], [0, 16]])[0:pcnt], op=ALU.mult), sv)
        P.wait("vector", sv)
        P.wait("vector", sv)
        P.op("vector", lambda e: e.tensor_scalar(
            out=QB_[0:pcnt, 0:ncols].bitcast(I32), in0=QXN[0:pcnt, 0:ncols].bitcast(I32),
            scalar1=EXP_MASK, scalar2=None, op0=ALU.bitwise_and), sv)
        P.wait("vector", sv)
        P.op("vector", lambda e: e.tensor_scalar(
            out=QB_[0:pcnt, 0:ncols], in0=QB_[0:pcnt, 0:ncols],
            scalar1=1.0, scalar2=64.0, op0=ALU.max, op1=ALU.mult), sv)
        P.op("vector", lambda e: e.tensor_scalar(
            out=qsg[0:pcnt, 0:ncols].bitcast(I32), in0=QXN[0:pcnt, 0:ncols].bitcast(I32),
            scalar1=SIGN_MASK, scalar2=None, op0=ALU.bitwise_and), sv)
        P.wait("vector", sv)
        P.op("vector", lambda e: e.tensor_tensor(
            out=QB_[0:pcnt, 0:ncols].bitcast(I32), in0=QB_[0:pcnt, 0:ncols].bitcast(I32),
            in1=qsg[0:pcnt, 0:ncols].bitcast(I32), op=ALU.bitwise_or), sv)
        P.wait("vector", sv)
        P.op("vector", lambda e: e.tensor_tensor(
            out=qtb[0:pcnt, 0:ncols], in0=QXN[0:pcnt, 0:ncols], in1=QB_[0:pcnt, 0:ncols],
            op=ALU.add), sv)
        P.wait("vector", sv)
        P.op("vector", lambda e: e.tensor_tensor(
            out=qtb[0:pcnt, 0:ncols], in0=qtb[0:pcnt, 0:ncols], in1=QB_[0:pcnt, 0:ncols],
            op=ALU.subtract), sv)
        P.wait("vector", sv)
        P.op("vector", lambda e: e.tensor_tensor(
            out=view(dst, 0, [[16, nb16], [1, 16]])[0:pcnt],
            in0=view(qtb[:], 0, [[16, nb16], [1, 16]])[0:pcnt],
            in1=view(qs1[:], 0, [[1, nb16], [0, 16]])[0:pcnt], op=ALU.mult), sv)
        P.wait("vector", sv)

    def wquant(SRC, DST, prow0, pcnt, col0, cw, dstc0):
        P.wait("sync", sv)
        P.op("sync", lambda e: e.dma_start(
            out=xf[0][0:pcnt, 0:cw].bitcast(F32) if False else xf[0][0:pcnt, 0:cw],
            in_=SRC[prow0:prow0 + pcnt, col0:col0 + cw]), sl, 16)
        P.wait("vector", sl)
        quant_tile(xf[0][:, 0:cw], wqtmp[:, 0:cw], pcnt, cw, qtbP, qsgP)
        P.wait("tensor", sv)
        for fc in range(cw // 128):
            trp = ps_tr if fc % 2 == 0 else ps_trB
            P.op("tensor", lambda e, fc=fc, trp=trp: e.transpose(
                out=trp[:], in_=wqtmp[0:128, fc * 128:(fc + 1) * 128], identity=identt[:]), sp)
            P.wait("vector", sp)
            P.op("vector", lambda e, fc=fc, trp=trp: e.tensor_copy(
                out=qtbP[:, 0:128], in_=trp[:, 0:128]), sv)
            P.wait("gpsimd", sv)
            P.op("gpsimd", lambda e, fc=fc: e.dma_start(
                out=DST[col0 + fc * 128:col0 + (fc + 1) * 128, dstc0:dstc0 + pcnt],
                in_=qtbP[0:128, 0:pcnt]), sl2, 16)
            P.wait("vector", sl2)
            P.wait("tensor", sl2, max(0, sl2.n - 16))

    for (SRC, DST) in ((WG_SL, WGQ_SL), (WU_SL, WUQ_SL)):
        for pc, pcnt in ((0, 128), (1, 128), (2, 96)):
            wquant(SRC, DST, pc * 128, pcnt, 0, 1024, pc * 128)
    for col0, cw in ((0, 1024), (1024, 1024), (2048, 768)):
        wquant(WD_SL, WDQ_SL, 0, 128, col0, cw, 0)
    P.wait("gpsimd", sl2)
    P.wait("tensor", sl2)
    P.op("gpsimd", lambda e: e.collective_compute(
        "AllGather", ALU.bypass, replica_groups=[list(range(NCORE))],
        ins=[WGQ_SL.opt()], outs=[WGQ.opt()]), scc, 1)
    P.mark(scc)
    P.op("gpsimd", lambda e: e.collective_compute(
        "AllGather", ALU.bypass, replica_groups=[list(range(NCORE))],
        ins=[WUQ_SL.opt()], outs=[WUQ.opt()]), scc, 1)
    P.mark(scc)
    P.op("gpsimd", lambda e: e.collective_compute(
        "AllGather", ALU.bypass, replica_groups=[list(range(NCORE))],
        ins=[WDQ_SL.opt()], outs=[WDQ.opt()]), scc, 1)
    P.mark(scc)
    bar()

    def emit_stats(xsrc_dram):
        P.wait("scalar", sp)
        P.wait("scalar", sv)
        P.wait("sync", sv)
        P.wait("sync", sa)
        for dmc in range(8):
            for hh in range(2):
                if dmc + hh > 0:
                    P.wait("sync", sa)
                P.op("sync", lambda e, dmc=dmc, hh=hh: e.dma_start(
                    out=xf[(2 * dmc + hh) % 2][:], in_=xsrc_dram(dmc, hh)), sl, 16)
                P.wait("scalar", sl)
                P.op("scalar", lambda e, dmc=dmc, hh=hh: e.activation(
                    out=scr[dmc // 4][:, (dmc % 4) * 2048 + hh * 1024:(dmc % 4) * 2048 + hh * 1024 + 1024],
                    in_=xf[(2 * dmc + hh) % 2][:], func=AF.Copy), sa)
        for nb in range(4):
            P.wait("tensor", sa)
            for dmc in range(8):
                off = (dmc % 4) * 2048 + nb * 512
                P.wait("scalar", sp)
                P.op("scalar", lambda e, dmc=dmc, off=off: e.activation(
                    out=sq5[dmc % 2][:], in_=scr[dmc // 4][:, off:off + 512], func=AF.Square), sa)
                P.op("tensor", lambda e, dmc=dmc, off=off: e.matmul(
                    out=ps_r1[0:1, 0:512], lhsT=onescol[:], rhs=scr[dmc // 4][:, off:off + 512],
                    start=(dmc == 0), stop=(dmc == 7)), sp)
                P.wait("tensor", sa)
                P.op("tensor", lambda e, dmc=dmc: e.matmul(
                    out=ps_r2[0:1, 0:512], lhsT=onescol[:], rhs=sq5[dmc % 2][:],
                    start=(dmc == 0), stop=(dmc == 7)), sp)
            P.wait("vector", sp)
            cb = slice(nb * 512, nb * 512 + 512)
            P.op("vector", lambda e, cb=cb: e.tensor_scalar(
                out=murow[0:1, cb], in0=ps_r1[0:1, 0:512], scalar1=1.0 / D, scalar2=None,
                op0=ALU.mult), sv)
            P.op("vector", lambda e, cb=cb: e.tensor_scalar(
                out=lnt1[0:1, cb], in0=ps_r2[0:1, 0:512], scalar1=1.0 / D, scalar2=None,
                op0=ALU.mult), sv)
            P.wait("tensor", sv)
        P.op("vector", lambda e: e.tensor_tensor(out=rhorow[:], in0=murow[:], in1=murow[:], op=ALU.mult), sv)
        P.op("vector", lambda e: e.tensor_tensor(out=lnt1, in0=lnt1, in1=rhorow[:], op=ALU.subtract), sv)
        P.wait("scalar", sv)
        P.op("scalar", lambda e: e.activation(out=lnt1, in_=lnt1, func=AF.Ln, bias=epsr[0:1, 0:1]), sa)
        P.op("scalar", lambda e: e.activation(out=rhorow[:], in_=lnt1, func=AF.Exp, scale=-0.5), sa)
        P.op("scalar", lambda e: e.activation(out=lnt1, in_=lnt1, func=AF.Exp, scale=0.5), sa)
        P.wait("vector", sa)
        P.op("vector", lambda e: e.tensor_copy(out=mub[:], in_=murow[:]), sv)
        P.op("vector", lambda e: e.tensor_copy(out=rhob[:], in_=rhorow[:]), sv)
        P.op("vector", lambda e: e.tensor_copy(out=isb_[:], in_=lnt1), sv)
        P.wait("tensor", sv)
        for nb in range(4):
            P.op("tensor", lambda e, nb=nb: e.matmul(
                out=ps_mm[:, 0:512], lhsT=onesrow[0:1, 0:128], rhs=rhob[0:1, nb * 512:nb * 512 + 512],
                start=True, stop=True), sp)
            P.wait("vector", sp)
            P.op("vector", lambda e, nb=nb: e.tensor_copy(
                out=rhobc[:, nb * 512:nb * 512 + 512], in_=ps_mm[:, 0:512]), sv)
            P.wait("tensor", sv)
        bar()

    def emit_qkv():
        for fi, (dst, scale) in enumerate(((qb, 1.0), (kb, 0.125), (vb, 1.0))):
            for nb in range(4):
                cb = slice(nb * 512, nb * 512 + 512)
                P.wait("tensor", sv)
                for dmc in range(8):
                    P.op("tensor", lambda e, fi=fi, dmc=dmc, nb=nb: e.matmul(
                        out=ps_mm[:, 0:512],
                        lhsT=wqb[:, dmc * 384 + fi * 128:dmc * 384 + fi * 128 + 128],
                        rhs=scr[dmc // 4][:, (dmc % 4) * 2048 + nb * 512:(dmc % 4) * 2048 + nb * 512 + 512],
                        start=(dmc == 0), stop=False), sp)
                P.op("tensor", lambda e, fi=fi, nb=nb: e.matmul(
                    out=ps_mm[:, 0:512], lhsT=rowsum_nb[0:1, fi * 128:fi * 128 + 128],
                    rhs=mub[0:1, nb * 512:nb * 512 + 512], start=False, stop=False), sp)
                P.op("tensor", lambda e, fi=fi, nb=nb: e.matmul(
                    out=ps_mm[:, 0:512], lhsT=bqkv_b[0:1, fi * 128:fi * 128 + 128],
                    rhs=isb_[0:1, nb * 512:nb * 512 + 512], start=False, stop=True), sp)
                P.wait("vector", sp)
                P.op("vector", lambda e, dst=dst, cb=cb, scale=scale: e.scalar_tensor_tensor(
                    out=dst[:, cb], in0=ps_mm[:, 0:512], scalar=scale, in1=rhobc[:, cb],
                    op0=ALU.mult, op1=ALU.mult), sv)
        bar()

    for b in range(NBLK):
        emit_stats(lambda dmc, hh, b=b: XT[dmc * 128:(dmc + 1) * 128,
                                           b * TB + hh * 1024:b * TB + hh * 1024 + 1024])
        emit_qkv()
        if debug:
            P.wait("gpsimd", sv)
            for srci, src in enumerate((qb, kb, vb)):
                P.op("gpsimd", lambda e, src=src, srci=srci, b=b: e.dma_start(
                    out=dbg["qkv"][srci * 128:srci * 128 + 128, b * TB:(b + 1) * TB], in_=src[:]), st_, 16)

        P.wait("tensor", sv)
        for s in range(C):
            for si, (src, dstP) in enumerate(((qb, qP), (kb, kP), (vb, vP))):
                trp = ps_tr if (s * 3 + si) % 2 == 0 else ps_trB
                P.op("tensor", lambda e, src=src, s=s, trp=trp: e.transpose(
                    out=trp[:], in_=view(src[:], s, [[C, 128]]), identity=identt[:]), sp)
                P.wait("vector", sp)
                P.op("vector", lambda e, dstP=dstP, s=s, trp=trp: e.tensor_copy(
                    out=dstP[:, s * 128:(s + 1) * 128], in_=trp[:, 0:128]), sv)
                P.wait("tensor", sv)
        bar()

        for nb in range(4):
            mrange = slice(nb * 512, nb * 512 + 512)
            P.wait("tensor", sv)
            P.op("tensor", lambda e, mrange=mrange: e.matmul(
                out=ps_y[:, 0:512], lhsT=wst_bd[:], rhs=qb[:, mrange], start=True, stop=False), sp)
            for n in range(C):
                P.op("vector", lambda e, nb=nb, n=n: e.tensor_tensor(
                    out=view(scr[0][:], n, [[16, 32], [1, 16 - n]]),
                    in0=view(qb[:], nb * 512 + n, [[16, 32], [1, 16 - n]]),
                    in1=view(kb[:], nb * 512, [[16, 32], [1, 16 - n]]), op=ALU.mult), sv)
                P.wait("tensor", sv)
                P.op("tensor", lambda e, n=n: e.matmul(
                    out=view(ps_z[:], n, [[16, 32], [1, 16 - n]]),
                    lhsT=pn_tab[n][:],
                    rhs=view(scr[0][:], n, [[16, 32], [1, 16 - n]]),
                    start=True, stop=True), sp)
                P.wait("vector", sp)
                P.op("vector", lambda e, nb=nb, n=n: e.scalar_tensor_tensor(
                    out=view(scr[1][:], n, [[16, 32], [1, 16 - n]]),
                    in0=view(ps_z[:], n, [[16, 32], [1, 16 - n]]),
                    scalar=1.0,
                    in1=view(vb[:], nb * 512, [[16, 32], [1, 16 - n]]),
                    op0=ALU.mult, op1=ALU.mult), sv)
                P.wait("tensor", sv)
                P.op("tensor", lambda e, n=n: e.matmul(
                    out=view(ps_y[:], n, [[16, 32], [1, 16 - n]]),
                    lhsT=identt[:],
                    rhs=view(scr[1][:], n, [[16, 32], [1, 16 - n]]),
                    start=False, stop=(n == C - 1)), sp)
            P.wait("vector", sp)
            P.op("vector", lambda e, mrange=mrange: e.tensor_copy(
                out=ysb[:, mrange], in_=ps_y[:, 0:512]), sv)
            P.wait("tensor", sv)
        bar()

        for s in range(C):
            vkdst = scr[1] if s == 0 else scr[0]
            for h in range(2):
                P.op("vector", lambda e, s=s, h=h, vkdst=vkdst: e.scalar_tensor_tensor(
                    out=view(vkdst[:], h * 4096, [[64, 64], [1, 64]]),
                    in0=view(vP[:], s * 128 + h * 64, [[1, 64], [0, 64]]),
                    scalar=1.0,
                    in1=view(kP[:], s * 128 + h * 64, [[0, 64], [1, 64]]),
                    op0=ALU.mult, op1=ALU.mult), sv)
            if s > 0:
                P.op("vector", lambda e: e.tensor_tensor(
                    out=scr[1][:], in0=scr[1][:], in1=rbc[:], op=ALU.mult), sv)
                P.op("vector", lambda e: e.tensor_tensor(
                    out=scr[1][:], in0=scr[1][:], in1=scr[0][:], op=ALU.add), sv)
        bar()

        for fs in range(64):
            trp = ps_tr if fs % 2 == 0 else ps_trB
            P.wait("tensor", sv)
            P.op("tensor", lambda e, fs=fs, trp=trp: e.transpose(
                out=trp[:], in_=scr[1][:, fs * 128:(fs + 1) * 128], identity=identt[:]), sp)
            P.wait("vector", sp)
            P.op("vector", lambda e, fs=fs: e.tensor_copy(
                out=sigext[:, 0:1], in_=carry[:, fs:fs + 1]), sv)
            P.op("vector", lambda e, fs=fs, trp=trp: e.tensor_tensor_scan(
                out=sigf[:], data0=rcmir[:, fs:fs + 1].broadcast_to([128, 128]),
                data1=trp[:, 0:128], initial=carry[:, fs:fs + 1],
                op0=ALU.mult, op1=ALU.add), sv)
            P.wait("vector", sv)
            P.op("vector", lambda e, fs=fs: e.tensor_copy(
                out=carry[:, fs:fs + 1], in_=sigf[:, 127:128]), sv)
            P.op("vector", lambda e: e.tensor_copy(out=sigext[:, 1:128], in_=sigf[:, 0:127]), sv)
            P.wait("tensor", sv)
            P.op("tensor", lambda e: e.transpose(
                out=ps_tr2[:], in_=sigext[:, 0:128], identity=identt[:]), sp)
            P.wait("vector", sp)
            P.op("vector", lambda e, fs=fs: e.tensor_copy(
                out=sigm[:, fs * 128:(fs + 1) * 128], in_=ps_tr2[:, 0:128]), sv)
        bar()
        if debug:
            P.wait("gpsimd", sv)
            P.op("gpsimd", lambda e, b=b: e.dma_start(
                out=dbg["sig"][b * 128:(b + 1) * 128, :], in_=sigm[:]), st_, 16)

        for c in range(C):
            if c == 0:
                P.op("vector", lambda e: e.tensor_tensor(
                    out=scr[1][:], in0=anc0[:], in1=sigm[:], op=ALU.mult), sv)
            else:
                P.op("vector", lambda e: e.tensor_tensor(
                    out=scr[1][:], in0=scr[1][:], in1=rbc[:], op=ALU.mult), sv)
            for h in range(2):
                P.op("vector", lambda e, c=c, h=h: e.tensor_tensor(
                    out=view(scr[0][:], h * 4096, [[64, 64], [1, 64]]),
                    in0=view(scr[1][:], h * 4096, [[64, 64], [1, 64]]),
                    in1=view(qP[:], c * 128 + h * 64, [[0, 64], [1, 64]]), op=ALU.mult), sv)
            for w in (32, 16, 8):
                P.op("vector", lambda e, w=w: e.tensor_tensor(
                    out=view(scr[0][:], 0, [[64, 128], [1, w]]),
                    in0=view(scr[0][:], 0, [[64, 128], [1, w]]),
                    in1=view(scr[0][:], w, [[64, 128], [1, w]]), op=ALU.add), sv)
            P.op("vector", lambda e, c=c: e.tensor_reduce(
                out=view(sigf[:], 0, [[1, 128], [1, 1]]),
                in_=view(scr[0][:], 0, [[64, 128], [1, 8]]),
                axis=AX.X, op=ALU.add), sv)
            P.wait("vector", sv)
            P.op("vector", lambda e, c=c: e.tensor_copy(
                out=z2s[:, c * 128:(c + 1) * 128], in_=sigf[:, 0:128]), sv)
        bar()

        for c in range(C):
            trp = ps_tr if c % 2 == 0 else ps_trB
            P.wait("tensor", sv)
            P.op("tensor", lambda e, c=c, trp=trp: e.transpose(
                out=trp[:], in_=z2s[:, c * 128:(c + 1) * 128], identity=identt[:]), sp)
            P.wait("vector", sp)
            P.op("vector", lambda e, c=c, trp=trp: e.tensor_tensor(
                out=view(ysb[:], c, [[16, 128]]),
                in0=view(ysb[:], c, [[16, 128]]),
                in1=trp[:, 0:128], op=ALU.add), sv)
        P.wait("vector", sl2)
        P.op("vector", lambda e: e.tensor_copy(out=z2s[:, 0:2048], in_=ysb[:]), sv)
        P.wait("gpsimd", sv)
        P.op("gpsimd", lambda e, b=b: e.dma_start(
            out=YSLICE[:, b * TB:(b + 1) * TB], in_=z2s[:, 0:2048]), sl2, 16)
        if debug:
            P.op("gpsimd", lambda e, b=b: e.dma_start(
                out=dbg["y"][:, b * TB:(b + 1) * TB], in_=z2s[:, 0:2048]), st_, 16)
        bar()

    P.wait("gpsimd", sl2)
    P.op("gpsimd", lambda e: e.collective_compute(
        "AllGather", ALU.bypass, replica_groups=[list(range(NCORE))],
        ins=[YSLICE.opt()], outs=[YGATH.opt()]), scc, 1)
    P.mark(scc)

    esA.close()
    esP.close()
    esB = ExitStack()
    sbB = mk(esB)
    ps_mm2 = psum("ps_mm2", [128, 352], F32, esB)
    ps_mmg = psum("ps_mmg", [128, 352], F32, esB)
    mubc = sbB("mubc", [128, TB], F32)
    rhobcf = sbB("rhobcf", [128, TB], F32)
    xnf = sbB("xnf", [128, TB], F32)
    xtokf = sbB("xtokf", [128, 128], F32)
    qtbB = sbB("qtbB", [128, 1024], BF16)
    qsgB = sbB("qsgB", [128, 1024], F32)
    hq = sbB("hq", [128, 24 * 256], BF16)
    sgt = sbB("sgt", [128, 352], BF16)
    htokf = sbB("htokf", [128, 352], F32)
    hqtok = sbB("hqtok", [128, 352], BF16)
    ygt = sbB("ygt", [128, 8 * 256], BF16)
    wgt = sbB("wgt", [128, 8 * 352], BF16)
    wut = sbB("wut", [128, 8 * 352], BF16)
    wdall = sbB("wdall", [128, 24 * 128], BF16)
    wotall = sbB("wotall", [128, 8 * 128], BF16)
    obuf = sbB("obuf", [128, 512], F32)
    xrs = sbB("xrs", [128, 512], F32)

    emit_stats(lambda dmc, hh: XTO[dmc * 128:(dmc + 1) * 128, hh * 1024:hh * 1024 + 1024])
    P.wait("gpsimd", sv)
    P.op("gpsimd", lambda e: e.dma_start(out=MRHO[0:1, :], in_=murow[:]), sl2, 16)
    P.op("gpsimd", lambda e: e.dma_start(out=MRHO[1:2, :], in_=rhorow[:]), sl2, 16)
    P.wait("gpsimd", sl2)
    P.op("gpsimd", lambda e: e.dma_start(out=mubc[:], in_=MRHO[0:1, :].broadcast_to([128, TB])), sl2, 16)
    P.op("gpsimd", lambda e: e.dma_start(out=rhobcf[:], in_=MRHO[1:2, :].broadcast_to([128, TB])), sl2, 16)
    P.wait("vector", sl2)
    bar()

    for dmc in range(8):
        for hh in range(2):
            P.wait("sync", sv)
            P.op("sync", lambda e, dmc=dmc, hh=hh: e.dma_start(
                out=xf[hh % 2][:], in_=XTO[dmc * 128:(dmc + 1) * 128, hh * 1024:hh * 1024 + 1024]), sl, 16)
            P.wait("vector", sl)
            P.op("vector", lambda e, dmc=dmc, hh=hh: e.tensor_tensor(
                out=xnf[:, hh * 1024:hh * 1024 + 1024], in0=xf[hh % 2][:],
                in1=mubc[:, hh * 1024:hh * 1024 + 1024], op=ALU.subtract), sv)
        P.op("vector", lambda e: e.tensor_tensor(
            out=xnf[:], in0=xnf[:], in1=rhobcf[:], op=ALU.mult), sv)
        for tc in range(16):
            P.wait("tensor", sv)
            P.op("tensor", lambda e, tc=tc: e.matmul(
                out=ps_mm[:, 0:128], lhsT=xnf[:, tc * 128:(tc + 1) * 128],
                rhs=identf[:, 0:128], start=True, stop=True), sp)
            P.wait("vector", sp)
            P.op("vector", lambda e: e.tensor_copy(out=xtokf[:], in_=ps_mm[:, 0:128]), sv)
            quant_tile(xtokf[:, 0:128], qtbB[:, 128:256], 128, 128, qtbB, qsgB)
            P.wait("tensor", sv)
            P.op("tensor", lambda e: e.transpose(
                out=ps_tr[:], in_=qtbB[:, 128:256], identity=identt[:]), sp)
            P.wait("vector", sp)
            P.op("vector", lambda e, dmc=dmc, tc=tc: e.tensor_copy(
                out=scr[dmc // 4][:, (dmc % 4) * 2048 + tc * 128:(dmc % 4) * 2048 + tc * 128 + 128],
                in_=ps_tr[:, 0:128]), sv)
            P.wait("tensor", sv)
    bar()
    if debug:
        P.wait("gpsimd", sv)
        for dmc in range(8):
            P.op("gpsimd", lambda e, dmc=dmc: e.dma_start(
                out=dbg["xq"][dmc * 128:(dmc + 1) * 128, :],
                in_=scr[dmc // 4][:, (dmc % 4) * 2048:(dmc % 4) * 2048 + 2048]), st_, 16)

    pid = nc.partition_id([mybir.EngineType.SP])
    P.wait("gpsimd", scc)
    P.wait("sync", scc)
    if debug:
        P.op("gpsimd", lambda e: e.dma_start(out=dbg["wdq"][:, :], in_=WDQ_SL[0:IFF, :]), st_, 16)
        P.wait("gpsimd", st_)
    NS = 8
    TS = 256
    for sub in range(NS):
        P.wait("sync", sp)
        P.op("sync", lambda e, sub=sub: e.dma_start(
            out=ygt[:].rearrange("p (r n) -> p r n", r=8),
            in_=dataclasses.replace(
                YGATH, offset=YGATH.offset + pid * TB + sub * TS,
                ap=[[L, 128], [128 * L, 8], [1, TS]])), sl, 16)
        if debug and sub == 0:
            P.wait("gpsimd", sl)
            P.op("gpsimd", lambda e: e.dma_start(out=dbg["ygt"][:, :], in_=ygt[:]), st_, 16)
            P.wait("sync", st_)
        for r in range(8):
            P.wait("sync", sp)
            P.op("sync", lambda e, r=r: e.dma_start(
                out=wgt[:].rearrange("p (c n) -> p c n", c=8),
                in_=WGQ[r * 1024:(r + 1) * 1024, :].rearrange("(c p) n -> p c n", p=128)), sl, 16)
            P.op("sync", lambda e, r=r: e.dma_start(
                out=wut[:].rearrange("p (c n) -> p c n", c=8),
                in_=WUQ[r * 1024:(r + 1) * 1024, :].rearrange("(c p) n -> p c n", p=128)), sl, 16)
            for tg in range(2):
                P.wait("tensor", sl)
                P.wait("tensor", sv)
                P.wait("tensor", sa)
                for dmc in range(8):
                    xoff = (dmc % 4) * 2048 + sub * TS + tg * 128
                    P.op("tensor", lambda e, dmc=dmc, xoff=xoff: e.matmul(
                        out=ps_mmg[:, 0:352], lhsT=scr[dmc // 4][:, xoff:xoff + 128],
                        rhs=wgt[:, dmc * 352:(dmc + 1) * 352],
                        start=(dmc == 0), stop=(dmc == 7)), sp)
                for dmc in range(8):
                    xoff = (dmc % 4) * 2048 + sub * TS + tg * 128
                    P.op("tensor", lambda e, dmc=dmc, xoff=xoff: e.matmul(
                        out=ps_mm2[:, 0:352], lhsT=scr[dmc // 4][:, xoff:xoff + 128],
                        rhs=wut[:, dmc * 352:(dmc + 1) * 352],
                        start=(dmc == 0), stop=(dmc == 7)), sp)
                P.wait("scalar", sp)
                P.op("scalar", lambda e: e.activation(out=sgt[:, 0:352], in_=ps_mmg[:, 0:352],
                                                      func=AF.Silu), sa)
                P.wait("vector", sa)
                P.op("vector", lambda e: e.scalar_tensor_tensor(
                    out=htokf[:, 0:352], in0=ps_mm2[:, 0:352], scalar=1.0, in1=sgt[:, 0:352],
                    op0=ALU.mult, op1=ALU.mult), sv)
                quant_tile(htokf[:, 0:352], hqtok[:, 0:352], 128, 352, qtbB, qsgB)
                P.wait("tensor", sv)
                for kk, kw in ((0, 128), (1, 128), (2, 96)):
                    trp = ps_tr if kk % 2 == 0 else ps_tr2
                    P.op("tensor", lambda e, kk=kk, kw=kw, trp=trp: e.transpose(
                        out=trp[0:kw, 0:128], in_=hqtok[:, kk * 128:kk * 128 + kw],
                        identity=identt[:]), sp)
                    P.wait("vector", sp)
                    slot = r * 3 + kk
                    P.op("vector", lambda e, slot=slot, tg=tg, kw=kw, trp=trp: e.tensor_copy(
                        out=hq[0:kw, slot * TS + tg * 128:slot * TS + tg * 128 + 128],
                        in_=trp[0:kw, 0:128]), sv)
                    P.wait("tensor", sv)
        bar()
        if debug and sub == 0:
            P.wait("gpsimd", sv)
            P.op("gpsimd", lambda e: e.dma_start(out=dbg["hq"][:, :], in_=hq[:]), st_, 16)

        for fr in range(8):
            P.wait("tensor", sv)
            P.wait("sync", sp)
            for r in (range(8) if PARTS["ffn"] else ()):
                for kk, kw in ((0, 128), (1, 128), (2, 96)):
                    slot = r * 3 + kk
                    P.op("sync", lambda e, fr=fr, r=r, kk=kk, kw=kw, slot=slot: e.dma_start(
                        out=wdall[0:kw, slot * 128:slot * 128 + 128],
                        in_=WDQ[fr * IFF + r * 352 + kk * 128:
                                fr * IFF + r * 352 + kk * 128 + kw, :]), sl, 16)
            for r in (range(8) if PARTS["attn"] else ()):
                P.op("sync", lambda e, r=r, fr=fr: e.dma_start(
                    out=wotall[:, r * 128:(r + 1) * 128],
                    in_=OWB16[r * 128:(r + 1) * 128, fr * 128:(fr + 1) * 128]), sl3, 16)
            P.wait("tensor", sl)
            P.wait("tensor", sl3)
            first = True
            for r in (range(8) if PARTS["ffn"] else ()):
                for kk, kw in ((0, 128), (1, 128), (2, 96)):
                    slot = r * 3 + kk
                    P.op("tensor", lambda e, slot=slot, kw=kw, first=first: e.matmul(
                        out=ps_y[:, 0:TS], lhsT=wdall[0:kw, slot * 128:slot * 128 + 128],
                        rhs=hq[0:kw, slot * TS:(slot + 1) * TS],
                        start=first, stop=False), sp)
                    first = False
            for r in (range(8) if PARTS["attn"] else ()):
                P.op("tensor", lambda e, r=r, first=first: e.matmul(
                    out=ps_y[:, 0:TS], lhsT=wotall[:, r * 128:(r + 1) * 128],
                    rhs=ygt[:, r * TS:(r + 1) * TS], start=first, stop=False), sp)
                first = False
            if PARTS["bias"]:
                P.op("tensor", lambda e, fr=fr, first=first: e.matmul(
                    out=ps_y[:, 0:TS], lhsT=outbb[0:1, fr * 128:fr * 128 + 128],
                    rhs=onesrow[0:1, 0:TS], start=first, stop=True), sp)
                first = False
            else:
                P.op("tensor", lambda e, fr=fr: e.matmul(
                    out=ps_y[:, 0:TS], lhsT=outbb[0:1, fr * 128:fr * 128 + 128],
                    rhs=onesrow[0:1, 0:TS], start=False, stop=True), sp) if False else None
            P.op("sync", lambda e, fr=fr, sub=sub: e.dma_start(
                out=xrs[:, 0:TS],
                in_=XTO[fr * 128:(fr + 1) * 128, sub * TS:sub * TS + TS]), sl3, 16)
            P.wait("vector", sp)
            P.wait("vector", sl3)
            P.wait("vector", st_)
            P.op("vector", lambda e: e.scalar_tensor_tensor(
                out=obuf[:, 0:TS], in0=ps_y[:, 0:TS], scalar=1.0, in1=xrs[:, 0:TS],
                op0=ALU.mult, op1=ALU.add), sv)
            P.wait("gpsimd", sv)
            P.op("gpsimd", lambda e, fr=fr, sub=sub: e.dma_start(
                out=OUT[fr * 128:(fr + 1) * 128, sub * TS:sub * TS + TS],
                in_=obuf[:, 0:TS]), st_, 16)
            P.wait("sync", sv)
        bar()

    P.wait("gpsimd", st_)
    P.wait("sync", st_)
    esB.close()

    with nc.Block() as block:
        @block.sync
        def _(e):
            P.replay("sync", e)

        @block.vector
        def _(e):
            P.replay("vector", e)

        @block.scalar
        def _(e):
            P.replay("scalar", e)

        @block.tensor
        def _(e):
            P.replay("tensor", e)

        @block.gpsimd
        def _(e):
            P.replay("gpsimd", e)

    es.close()
    return nc


_prog_cache = {}


def get_program(debug=False):
    if debug not in _prog_cache:
        _prog_cache[debug] = build_program(debug)
    return _prog_cache[debug]


def make_inmaps(inputs):
    x = np.asarray(inputs["L0_attn_x"])[0]
    xT = np.ascontiguousarray(x.T)
    wqkv = np.asarray(inputs["L0_attn_Wqkv_w"])
    bqkv = np.asarray(inputs["L0_attn_Wqkv_b"])
    outw = np.asarray(inputs["L0_attn_out_w"])
    outb = np.asarray(inputs["L0_attn_out_b"])
    wst = np.asarray(inputs["L0_attn_W_static"])
    lam = np.asarray(inputs["L0_attn_Lambda_raw"])
    gam = np.asarray(inputs["L0_attn_Gamma"])
    wg = np.asarray(inputs["L0_ffn_w_gate"])
    wu = np.asarray(inputs["L0_ffn_w_up"])
    wd = np.asarray(inputs["L0_ffn_w_down"])
    ident = np.eye(128, dtype=np.float32)
    wqkvT = np.ascontiguousarray(wqkv.T)
    outwT = np.ascontiguousarray(outw.T)

    maps = []
    for k in range(NCORE):
        lam_ij = np.ascontiguousarray(lam[2 * k:2 * k + 2].reshape(128, 64))
        gam_ij = np.ascontiguousarray(gam[2 * k:2 * k + 2].reshape(128, 64))
        lam_ji = np.ascontiguousarray(np.transpose(lam[2 * k:2 * k + 2], (0, 2, 1)).reshape(128, 64))
        gam_ji = np.ascontiguousarray(np.transpose(gam[2 * k:2 * k + 2], (0, 2, 1)).reshape(128, 64))
        wst_ji = np.ascontiguousarray(np.transpose(wst[2 * k:2 * k + 2], (0, 2, 1)).reshape(128, 64))
        cols = np.concatenate([np.arange(2 * k * 64, 2 * k * 64 + 128),
                               1024 + np.arange(2 * k * 64, 2 * k * 64 + 128),
                               2048 + np.arange(2 * k * 64, 2 * k * 64 + 128)])
        maps.append({
            "xt": xT,
            "xto": np.ascontiguousarray(xT[:, k * TB:(k + 1) * TB]),
            "wqkvt": np.ascontiguousarray(wqkvT[:, cols]),
            "bqkv": np.ascontiguousarray(bqkv[cols]).reshape(1, 384),
            "outwt": outwT,
            "outb": outb.reshape(1, D),
            "lam_ij": lam_ij, "gam_ij": gam_ij,
            "lam_ji": lam_ji, "gam_ji": gam_ji,
            "wstat_ji": wst_ji,
            "wg_sl": np.ascontiguousarray(wg[k * 352:(k + 1) * 352, :]),
            "wu_sl": np.ascontiguousarray(wu[k * 352:(k + 1) * 352, :]),
            "wd_sl": np.ascontiguousarray(wd[k * 128:(k + 1) * 128, :]),
            "ident": ident,
        })
    return maps


def kernel(**inputs):
    nc = get_program(debug=False)
    maps = make_inmaps(inputs)
    res = run_bass_kernel_spmd(nc, maps, core_ids=list(range(NCORE)))
    outs = [np.asarray(res.results[k]["outx"], np.float32).T for k in range(NCORE)]
    return np.concatenate(outs, axis=0)[None]
